# version 09
# baseline (speedup 1.0000x reference)
import numpy as np
from contextlib import ExitStack
import concourse.bass as bass
import concourse.mybir as mybir
from concourse.bass_utils import run_bass_kernel_spmd

F32 = mybir.dt.float32
BF16 = mybir.dt.bfloat16
F8 = mybir.dt.float8e4
AF = mybir.ActivationFunctionType
ALU = mybir.AluOpType
AX = mybir.AxisListType

T = 2048
HID = 2048
C = 128
NCH = T // C
NK = 16
P2 = 2
NV = 32
DK = 128
DV = 128
NDT = HID // 128
FQ = 64
FZ = 32
NTB = 4
TB = 512

ENGINES = ("sync", "gpsimd", "tensor", "vector", "scalar")


class Sched:

    def __init__(self):
        self.ops = {e: [] for e in ENGINES}
        self.done = {e: 0 for e in ENGINES}
        self.slots = {}

    def emit(self, eng, fn, waits=(), inc=0, slot=None):
        ws = [w for w in waits if w is not None]
        self.ops[eng].append((fn, ws, inc, slot))
        if slot is not None:
            self.slots[slot] = self.slots.get(slot, 0) + 16
            self.slot_eng = getattr(self, "slot_eng", {})
            self.slot_eng[slot] = eng
            return (slot, self.slots[slot])
        if inc:
            self.done[eng] += inc
            return (eng, self.done[eng])
        return None

    def barrier_token(self):
        toks = [(e, self.done[e]) for e in ENGINES if self.done[e] > 0]
        toks += [(sl, v) for sl, v in self.slots.items()]
        return toks

    SELF_WAIT_ENGINES = ("scalar", "gpsimd", "vector")

    def replay(self, engname, engine, sems, self_waits=False):
        hw = {}
        own = -1
        for fn, waits, inc, slot in self.ops[engname]:
            for peng, val in waits:
                if peng == engname:
                    if (self_waits or engname in self.SELF_WAIT_ENGINES) and val > own:
                        engine.wait_ge(sems[peng], val)
                        own = val
                    continue
                if hw.get(peng, 0) >= val:
                    continue
                engine.wait_ge(sems[peng], val)
                hw[peng] = val
            instr = fn(engine)
            if slot is not None:
                instr.then_inc(sems[slot], 16)
            elif inc:
                instr.then_inc(sems[engname], inc)


def build_nc(debug=False, detect_races=False, self_waits=False, only00=False):
    nc = bass.Bass("TRN2", target_bir_lowering=False, detect_race_conditions=detect_races)

    di = {}
    def inp(name, shape, dtype=F32):
        di[name] = nc.dram_tensor(name, shape, dtype, kind="ExternalInput").ap()
        return di[name]

    x_nat = inp("x_nat", [T, HID])
    xT = inp("xT", [HID, T])
    wqkv_l = inp("wqkv_l", [FQ, 128, HID])
    wz_l = inp("wz_l", [FZ, 128, HID])
    wab_l = inp("wab_l", [128, NDT * 64])
    wout_l = inp("wout_l", [4, 32, 128, 512])
    in2 = inp("in2", [HID, HID])
    convw_in = inp("convw_in", [128, FQ * 4])
    dtbias_bc_in = inp("dtbias_bc", [128, 32])
    alog_bc_in = inp("alog_bc", [128, 32])
    normw_bc_in = inp("normw_bc", [128, 128])
    normw_col_in = inp("normw_col", [128, 1])
    scale_col_in = inp("scale_col", [128, 1])
    scale1_col_in = inp("scale1_col", [128, 1])
    tri_lu_in = inp("tri_lu", [128, 128])
    tri_li_in = inp("tri_li", [128, 128])
    ident_in = inp("ident", [128, 128])

    y_out = nc.dram_tensor("y", [T, HID], F32, kind="ExternalOutput").ap()

    sk = "ExternalOutput" if debug else "Internal"
    qkvc_sp = nc.dram_tensor("qkvc_sp", [NCH, 128, FQ * 128], F32, kind=sk).ap()
    gate_sp = nc.dram_tensor("gate_sp", [NCH, 128, FZ * 128], F32, kind=sk).ap()
    outT_sp = nc.dram_tensor("outT_sp", [NCH, 128, 2 * FZ * 128], BF16, kind=sk).ap()
    dbg = None
    if debug:
        dbg = nc.dram_tensor("dbg", [128, 4096], F32, kind="ExternalOutput").ap()
        ssq_dbg = nc.dram_tensor("ssq_dbg", [1, T], F32, kind="ExternalOutput").ap()
        var_dbg = nc.dram_tensor("var_dbg", [1, T], F32, kind="ExternalOutput").ap()
        rec_dbg = nc.dram_tensor("rec_dbg", [1, T], F32, kind="ExternalOutput").ap()
        musq_dbg = nc.dram_tensor("musq_dbg", [1, T], F32, kind="ExternalOutput").ap()
        xnT_dbg = nc.dram_tensor("xnT_dbg", [128, NDT * T], F32, kind="ExternalOutput").ap()

    es = ExitStack()

    ARENA_BYTES = 164352
    arena_h = nc.alloc_sbuf_tensor("arena", [128, ARENA_BYTES], mybir.dt.uint8)
    AR = nc.lookup_mloc(arena_h).addr
    SBUF_TOP = AR + ARENA_BYTES

    def alloc(name, shape, dtype=F32, at=None):
        if at is None:
            return nc.alloc_sbuf_tensor(name, shape, dtype).ap()
        nbytes = int(np.prod(shape[1:])) * mybir.dt.size(dtype)
        assert at + nbytes <= SBUF_TOP, (name, at, nbytes)
        return nc.alloc_sbuf_tensor_at(name, shape, dtype, offset=at).ap()

    ident = alloc("identt", [128, 128])
    tri_lu = alloc("tri_lut", [128, 128])
    tri_li = alloc("tri_lit", [128, 128])
    normw_bc = alloc("normw_bct", [128, 128])
    normw_col = alloc("normw_colt", [128, 1])
    convw = alloc("convwt", [128, FQ * 4])
    dtbias_bc = alloc("dtbias_bct", [128, 32])
    alog_bc = alloc("alog_bct", [128, 32])
    negexpA = alloc("negexpA", [128, 32])
    scale_col = alloc("scale_colt", [128, 1])
    u8scr = alloc("u8scr", [128, 4], mybir.dt.uint8)
    scale1_col = alloc("scale1_colt", [128, 1])
    rscale_col = alloc("rscale_col", [128, 1])
    tmp_col = alloc("tmp_col", [128, 1])
    ones_col = alloc("ones_col", [128, 1])
    ones_row = alloc("ones_row", [1, 128])
    ones_full = alloc("ones_full", [128, 128])
    one_col = alloc("one_col", [128, 1])
    beta_tm = alloc("beta_tm", [128, NCH * 32])
    ld_tm = alloc("ld_tm", [128, NCH * 32])
    xn8 = alloc("xn8", [128, NDT * T], F8)
    sp_scr = alloc("sp_scr", [128, 64])
    wsp_h = alloc("wsp_h", [128, 512], BF16)
    wsp_l = alloc("wsp_l", [128, 512], BF16)

    xnT = alloc("xnT", [128, NDT * T], at=AR)
    xn_hl = alloc("xn_hl", [128, 2 * NDT * T], BF16, at=AR)
    SB = AR + NDT * T * 4
    sqbuf4 = alloc("sqbuf4", [128, TB], at=SB)
    murow = alloc("murow", [1, TB], at=SB + 2048)
    isrow = alloc("isrow", [1, TB], at=SB + 4096)
    musq_scr = alloc("musq_scr", [1, TB], at=SB + 6144)
    mu_bc = alloc("mu_bc", [128, TB], at=SB + 8192)
    is_bc = alloc("is_bc", [128, TB], at=SB + 10240)
    hi_scr = alloc("hi_scr", [128, TB], BF16, at=SB + 12288)
    lo_scr = alloc("lo_scr", [128, TB], BF16, at=SB + 13312)
    ph0_end = SB + 14336
    convstage = alloc("convstage", [128, T + 4], at=SB)
    wab_hl = alloc("wab_hl", [128, 2 * NDT * 64], BF16, at=SB)
    gatebuf = alloc("gatebuf", [128, T], at=SB + 8320)
    wblk = [alloc(f"wblk{i}", [128, HID], at=SB + 16512 + i * HID * 4) for i in range(2)]
    wblk_hl = [alloc(f"wblk_hl{i}", [128, 2 * HID], BF16, at=SB + 16512 + i * HID * 4)
               for i in range(2)]
    ph1_end = SB + 32896
    assert max(ph0_end, ph1_end) <= SBUF_TOP, (ph0_end, ph1_end)

    ch_sb = alloc("ch_sb", [128, FQ * 128], at=AR)

    b1 = AR + FQ * 128 * 4
    gate_ch = alloc("gate_ch", [128, FZ * 128], at=b1)
    b2 = b1 + FZ * 128 * 4
    outT_ch = alloc("outT_ch", [128, FZ * 128], at=b2)
    outT_ch_hl = alloc("outT_ch_hl", [128, 2 * FZ * 128], BF16, at=b2)
    b3 = b2 + FZ * 128 * 4
    S_sb = alloc("S_sb", [128, FZ * 128], at=b3)
    b4 = b3 + FZ * 128 * 4
    O_all = alloc("O_all", [128, FZ * 128], at=b4)
    b5 = b4 + FZ * 128 * 4
    smalls_p = [{}, {}]
    off = b5
    for par in range(2):
        for nm in ["L_tm", "Lm1_tm", "gm1_tm", "neggm1_tm", "negb_ch",
                   "colscalb_tm", "glast_bc", "ssq_tm", "rms_tm"]:
            smalls_p[par][nm] = alloc(f"{nm}_{par}", [128, 32], at=off); off += 512
    rn_row = alloc("rn_row", [1, 4096], at=off); off += 16384
    sqh = alloc("sqh", [128, 128], at=off); off += 512
    HT = [[{}, {}], [{}, {}]]
    for pp in range(2):
        for hi in range(2):
            for nm in ["Kn", "Qn", "Kt", "G", "QG"]:
                HT[pp][hi][nm] = alloc(f"ht_{nm}_{pp}{hi}", [128, 128], at=off); off += 512
    CT = [{} for _ in range(4)]
    for ci in range(4):
        for nm in ["Et_a", "Et_b", "Etm_a", "Etm_b", "Y_a", "Y_b", "ArgMt", "Expb",
                   "Tmp", "VtWsc"]:
            CT[ci][nm] = alloc(f"ct_{nm}_{ci}", [128, 128], at=off); off += 512
    sq_all = alloc("sq_all", [128, 2048], at=off); off += 8192
    ph2_end = off
    assert ph2_end <= SBUF_TOP, ph2_end

    outT_t = [alloc(f"outT_t{i}", [128, 2 * FZ * 128], BF16, at=AR + i * FZ * 128 * 4)
              for i in range(2)]
    c1 = AR + 2 * FZ * 128 * 4
    woutc_hl = alloc("woutc_hl", [128, 2 * 32 * 512], BF16, at=c1)
    c2 = c1 + 32 * 512 * 4
    q28 = alloc("q28", [128, NDT * HID], F8, at=c2)
    c2b = c2 + NDT * HID
    in2buf = [alloc(f"in2buf{i}", [128, HID], at=c2b + i * HID * 4) for i in range(2)]
    c3 = c2b + 2 * HID * 4
    xbuf = [alloc(f"xbuf{i}", [128, 512], at=c3 + i * 2048) for i in range(2)]
    c4 = c3 + 2 * 2048
    fbuf = [alloc(f"fbuf{i}", [128, 512], at=c4 + i * 2048) for i in range(2)]
    c5 = c4 + 2 * 2048
    ybuf = [alloc(f"ybuf{i}", [128, 512], at=c5 + i * 2048) for i in range(2)]
    c6 = c5 + 2 * 2048
    wout_in = [alloc(f"wout_in{i}", [128, 512], at=c6 + i * 2048) for i in range(2)]
    ph3_end = c6 + 2 * 2048
    assert ph3_end <= SBUF_TOP, ph3_end

    ps = [es.enter_context(nc.psum_tensor(f"ps{i}", [128, 512], F32)) for i in range(8)]

    sems = {}
    for e in ENGINES:
        sems[e] = es.enter_context(nc.semaphore(f"sem_{e}"))

    S = Sched()

    def dma_in(dst, src, slot, waits=()):
        return S.emit("sync", lambda e, d=dst, s=src: e.dma_start(out=d, in_=s),
                      waits=waits, slot=slot)

    def dma_out(dst, src, slot, waits=()):
        return S.emit("gpsimd", lambda e, d=dst, s=src: e.dma_start(out=d, in_=s),
                      waits=waits, slot=slot)

    def mm(pdst, lhsT, rhs, start, stop, waits=(), inc=0):
        return S.emit("tensor",
                      lambda e, o=pdst, l=lhsT, r=rhs, st=start, sp=stop:
                      e.matmul(o, l, r, start=st, stop=sp, skip_group_check=True),
                      waits=waits, inc=inc)

    def transp(pdst, src, waits=(), inc=0):
        return S.emit("tensor",
                      lambda e, o=pdst, s=src: e.transpose(o, s, ident),
                      waits=waits, inc=inc)

    def ve(fn, waits=(), inc=0):
        return S.emit("vector", fn, waits=waits, inc=inc)

    def se(fn, waits=(), inc=0):
        return S.emit("scalar", fn, waits=waits, inc=inc)

    t_consts = []
    t_consts.append(dma_in(ident[:, :], ident_in[:, :], "cst"))
    t_consts.append(dma_in(tri_lu[:, :], tri_lu_in[:, :], "cst"))
    t_consts.append(dma_in(tri_li[:, :], tri_li_in[:, :], "cst"))
    t_consts.append(dma_in(normw_bc[:, :], normw_bc_in[:, :], "cst"))
    t_consts.append(dma_in(normw_col[:, :], normw_col_in[:, :], "cst"))
    t_consts.append(dma_in(convw[:, :], convw_in[:, :], "cst"))
    t_consts.append(dma_in(dtbias_bc[:, :], dtbias_bc_in[:, :], "cst"))
    t_consts.append(dma_in(alog_bc[:, :], alog_bc_in[:, :], "cst"))
    t_consts.append(dma_in(scale_col[:, :], scale_col_in[:, :], "cst"))
    t_consts.append(dma_in(scale1_col[:, :], scale1_col_in[:, :], "cst"))
    t_cst0 = t_consts[-1]
    t_tc1 = ve(lambda e: e.tensor_copy(u8scr[:, :], scale_col[:, :].bitcast(mybir.dt.uint8)),
               waits=[t_cst0], inc=1)
    t_cst = ve(lambda e: e.tensor_copy(scale_col[:, :].bitcast(mybir.dt.uint8), u8scr[:, :]),
               waits=[t_tc1], inc=1)
    t_ones = ve(lambda e: e.memset(ones_col[:, :], 1.0), inc=1)
    t_ones2 = ve(lambda e: e.memset(ones_row[:, :], 1.0), inc=1)
    t_ones3 = ve(lambda e: e.memset(ones_full[:, :], 1.0), inc=1)
    t_one = ve(lambda e: e.memset(one_col[:, :], 1.0), inc=1)
    t_expA = se(lambda e: e.activation(negexpA[:, :], alog_bc[:, :], AF.Exp),
                waits=[t_cst], inc=1)
    t_negA = ve(lambda e: e.tensor_scalar_mul(negexpA[:, :], negexpA[:, :], -1.0),
                waits=[t_expA], inc=1)
    t_r1 = ve(lambda e: e.reciprocal(rscale_col[:, :], scale_col[:, :]),
              waits=[t_cst], inc=1)
    t_r2 = ve(lambda e: e.reciprocal(tmp_col[:, :], scale1_col[:, :]),
              waits=[t_cst], inc=1)
    t_rs = ve(lambda e: e.tensor_mul(rscale_col[:, :], rscale_col[:, :], tmp_col[:, :]),
              waits=[t_r1, t_r2], inc=1)

    t_xT_all = None
    for d in range(NDT):
        t_xT_all = dma_in(xnT[:, d * T:(d + 1) * T], xT[d * 128:(d + 1) * 128, :], "xT")

    t_mu_mm = None
    t_sq_mm = None
    for d in range(NDT):
        for tb in range(NTB):
            lastd = (d == NDT - 1)
            t_mu_mm = mm(ps[tb][:1, :], ones_col[:, :], xnT[:, d * T + tb * TB: d * T + (tb + 1) * TB],
                         start=(d == 0), stop=lastd,
                         waits=[t_xT_all, t_ones], inc=1 if (lastd and tb == NTB - 1) else 0)
        for tb in range(NTB):
            t_sqa = se(lambda e, d=d, tb=tb: e.activation(
                sqbuf4[:, :], xnT[:, d * T + tb * TB: d * T + (tb + 1) * TB], AF.Square),
                waits=[t_xT_all, t_sq_mm], inc=1)
            t_sq_mm = mm(ps[4 + tb][:1, :], ones_col[:, :], sqbuf4[:, :],
                         start=(d == 0), stop=(d == NDT - 1),
                         waits=[t_sqa], inc=1)
    t_xn = [None] * NDT
    t_xn8 = [None] * NDT
    t_prev_blk = None
    for tb in range(NTB):
        t_mur = ve(lambda e, tb=tb: e.tensor_scalar(murow[:1, :], ps[tb][:1, :],
                                                    1.0 / HID, None, ALU.mult, ALU.bypass),
                   waits=[t_mu_mm, t_prev_blk], inc=1)
        t1 = ve(lambda e, tb=tb: e.tensor_scalar(isrow[:1, :], ps[4 + tb][:1, :],
                                                 1.0 / HID, None, ALU.mult, ALU.bypass),
                waits=[t_sq_mm], inc=1)
        t2 = ve(lambda e: e.scalar_tensor_tensor(musq_scr[:1, :], murow[:1, :], 1.0,
                                                 murow[:1, :], ALU.mult, ALU.mult),
                waits=[t_mur, t1], inc=1)
        t3 = ve(lambda e: e.tensor_sub(isrow[:1, :], isrow[:1, :], musq_scr[:1, :]),
                waits=[t2], inc=1)
        t4 = ve(lambda e: e.tensor_scalar_add(isrow[:1, :], isrow[:1, :], 1e-5),
                waits=[t3], inc=1)
        t4b = ve(lambda e: e.reciprocal(musq_scr[:1, :], isrow[:1, :]), waits=[t4], inc=1)
        t_isr = se(lambda e: e.activation(isrow[:1, :], musq_scr[:1, :], AF.Sqrt),
                   waits=[t4b], inc=1)
        tmm1 = mm(ps[tb][:, :], ones_row[:, :], murow[:1, :], start=True, stop=True,
                  waits=[t_mur, t_ones2, t_mu_mm, t2], inc=1)
        t_mubc = se(lambda e, tb=tb: e.copy(mu_bc[:, :], ps[tb][:, :]), waits=[tmm1], inc=1)
        tmm2 = mm(ps[4 + tb][:, :], ones_row[:, :], isrow[:1, :], start=True, stop=True,
                  waits=[t_isr], inc=1)
        t_isbc = se(lambda e, tb=tb: e.copy(is_bc[:, :], ps[4 + tb][:, :]), waits=[tmm2], inc=1)
        if debug:
            dma_out(dbg[:, tb * TB:(tb + 1) * TB], mu_bc[:, :], "dbgd", waits=[t_mubc])
            dma_out(dbg[:, T + tb * TB:T + (tb + 1) * TB], is_bc[:, :], "dbgd", waits=[t_isbc])
        blk_done = None
        for d in range(NDT):
            sl = (d * T + tb * TB, d * T + (tb + 1) * TB)
            tsub = ve(lambda e, sl=sl: e.tensor_sub(xnT[:, sl[0]:sl[1]],
                                                    xnT[:, sl[0]:sl[1]], mu_bc[:, :]),
                      waits=[t_mubc, t_sq_mm if tb == 0 else None], inc=1)
            tmul = ve(lambda e, sl=sl: e.tensor_mul(xnT[:, sl[0]:sl[1]],
                                                    xnT[:, sl[0]:sl[1]], is_bc[:, :]),
                      waits=[tsub, t_isbc], inc=1)
            t8 = ve(lambda e, sl=sl, d=d, tb=tb: e.tensor_scalar(
                xn8[:, sl[0]:sl[1]], xnT[:, sl[0]:sl[1]], scale_col[:, :], None,
                ALU.mult, ALU.bypass),
                waits=[tmul, t_cst], inc=1)
            th_ = ve(lambda e, sl=sl: e.tensor_copy(hi_scr[:, :], xnT[:, sl[0]:sl[1]]),
                     waits=[t8], inc=1)
            tl_ = ve(lambda e, sl=sl: e.tensor_sub(lo_scr[:, :], xnT[:, sl[0]:sl[1]],
                                                   hi_scr[:, :]),
                     waits=[th_], inc=1)
            tch = ve(lambda e, d=d, tb=tb: e.tensor_copy(
                xhi(d, tb * TB, TB), hi_scr[:, :]), waits=[tl_], inc=1)
            tcl2 = ve(lambda e, d=d, tb=tb: e.tensor_copy(
                xlo(d, tb * TB, TB), lo_scr[:, :]), waits=[tch], inc=1)
            if tb == NTB - 1:
                t_xn[d] = tmul
                t_xn8[d] = t8
            blk_done = tcl2
        t_prev_blk = blk_done
    txn_all = t_prev_blk

    if debug:
        dma_out(xnT_dbg[:, :], xnT[:, :], "dbgd", waits=[txn_all])

    def xhi(d, t0, w):
        blk = t0 // TB
        o = 2 * (d * T + blk * TB) + (t0 - blk * TB)
        return xn_hl[:, o:o + w]

    def xlo(d, t0, w):
        blk = t0 // TB
        o = 2 * (d * T + blk * TB) + TB + (t0 - blk * TB)
        return xn_hl[:, o:o + w]

    def whl_hi(buf, d):
        o = 1024 * (d // 4) + (d % 4) * 128
        return wblk_hl[buf][:, o:o + 128]

    def whl_lo(buf, d):
        o = 1024 * (d // 4) + 512 + (d % 4) * 128
        return wblk_hl[buf][:, o:o + 128]

    t_ld = [None] * NCH
    ph0_bar = S.barrier_token()
    t_wab = dma_in(wblk[0][:, :NDT * 64], wab_l[:, :], "wb0", waits=ph0_bar)
    NAB = NDT * 64
    t_wabh = ve(lambda e: e.tensor_copy(wab_hl[:, 0:NAB], wblk[0][:, 0:NAB]),
                waits=[t_wab] + ph0_bar, inc=1)
    t_wabl = ve(lambda e: e.tensor_sub(wab_hl[:, NAB:2 * NAB], wblk[0][:, 0:NAB],
                                       wab_hl[:, 0:NAB]),
                waits=[t_wabh], inc=1)
    t_ab_ps = [None] * NCH
    t_beta = [None] * NCH
    t_ta = [None] * NCH
    for ti in range(NCH):
        for d in range(NDT):
            w0 = ([t_wabl, txn_all,
                   t_beta[ti - 2] if ti >= 2 else None,
                   t_ta[ti - 2] if ti >= 2 else None]
                  if d == 0 else [txn_all])
            for pi, (lh, rh) in enumerate([
                    (xhi(d, ti * 128, 128), wab_hl[:, d * 64:(d + 1) * 64]),
                    (xhi(d, ti * 128, 128), wab_hl[:, NAB + d * 64:NAB + (d + 1) * 64]),
                    (xlo(d, ti * 128, 128), wab_hl[:, d * 64:(d + 1) * 64])]):
                t_ab_ps[ti] = mm(ps[ti % 2][:, 0:64], lh, rh,
                                 start=(d == 0 and pi == 0),
                                 stop=(d == NDT - 1 and pi == 2),
                                 waits=w0 if pi == 0 else [],
                                 inc=1 if (d == NDT - 1 and pi == 2) else 0)
        t_beta[ti] = se(lambda e, ti=ti: e.activation(beta_tm[:, ti * 32:(ti + 1) * 32],
                                                      ps[ti % 2][:, 0:32], AF.Sigmoid),
                        waits=[t_ab_ps[ti]], inc=1)
        t_ta[ti] = ve(lambda e, ti=ti: e.tensor_add(ld_tm[:, ti * 32:(ti + 1) * 32],
                                                    ps[ti % 2][:, 32:64], dtbias_bc[:, :]),
                      waits=[t_ab_ps[ti], t_beta[ti]], inc=1)
    t_adump = None
    if debug:
        t_adump = dma_out(dbg[:, 2048:2080], ld_tm[:, 0:32], "dbgd", waits=[t_ta[0]])
    for ti in range(NCH):
        sl = (ti * 32, (ti + 1) * 32)
        ta = t_ta[ti]
        if ti == 0 and t_adump is not None:
            ta = t_adump
        tb_ = se(lambda e, ti=ti, sl=sl: e.activation(sp_scr[:, 0:32], ld_tm[:, sl[0]:sl[1]], AF.Abs),
                 waits=[ta, t_ld[ti - 1] if ti >= 1 else None], inc=1)
        tc = se(lambda e: e.activation(sp_scr[:, 32:64], sp_scr[:, 0:32], AF.Exp, scale=-1.0),
                waits=[tb_], inc=1)
        td = se(lambda e: e.activation(sp_scr[:, 0:32], sp_scr[:, 32:64], AF.Ln, bias=one_col[:, :]),
                waits=[tc, t_one], inc=1)
        te_ = ve(lambda e, ti=ti, sl=sl: e.tensor_scalar(sp_scr[:, 32:64], ld_tm[:, sl[0]:sl[1]],
                                                         0.0, None, ALU.max, ALU.bypass),
                 waits=[td], inc=1)
        tf = ve(lambda e: e.tensor_add(sp_scr[:, 0:32], sp_scr[:, 0:32], sp_scr[:, 32:64]),
                waits=[te_], inc=1)
        t_ld[ti] = ve(lambda e, ti=ti, sl=sl: e.tensor_mul(ld_tm[:, sl[0]:sl[1]],
                                                           sp_scr[:, 0:32], negexpA[:, :]),
                      waits=[tf, t_negA], inc=1)

    t_wdma = [None, None]
    t_wfree = [t_ab_ps[NCH - 1], None]
    t_whl = [None, None]
    wst = {"prev": None}
    t_convdma = []
    t_spill_last = None
    t_gate_last = None
    ph1_mm = None
    t_evac = {}

    def emit_wload(fb):
        buf = fb % 2
        src = wqkv_l[fb, :, :] if fb < FQ else wz_l[fb - FQ, :, :]
        t_wdma[buf] = dma_in(wblk[buf][:, :], src, f"wb{buf}",
                             waits=[t_wfree[buf], t_wabl if fb < 2 else None])
        last = None
        for j in range(4):
            th = ve(lambda e, buf=buf, j=j: e.tensor_copy(
                wsp_h[:, :], wblk[buf][:, j * 512:(j + 1) * 512]),
                waits=[t_wdma[buf], wst["prev"]] if j == 0 else [last], inc=1)
            tl = ve(lambda e, buf=buf, j=j: e.tensor_sub(
                wsp_l[:, :], wblk[buf][:, j * 512:(j + 1) * 512], wsp_h[:, :]),
                waits=[th], inc=1)
            tch2 = ve(lambda e, buf=buf, j=j: e.tensor_copy(
                wblk_hl[buf][:, 1024 * j:1024 * j + 512], wsp_h[:, :]),
                waits=[tl], inc=1)
            last = ve(lambda e, buf=buf, j=j: e.tensor_copy(
                wblk_hl[buf][:, 1024 * j + 512:1024 * j + 1024], wsp_l[:, :]),
                waits=[tch2], inc=1)
        t_whl[buf] = last
        wst["prev"] = last

    emit_wload(0)
    for fb in range(FQ + FZ):
        buf = fb % 2
        is_q = fb < FQ
        zi = fb - FQ
        if fb + 1 < FQ + FZ:
            emit_wload(fb + 1)
        tb_mm = [None] * NTB
        for tb in range(NTB):
            for d in range(NDT):
                w = [t_whl[buf]] if d == 0 else []
                if d == 0:
                    w.append(txn_all)
                    w.append(t_evac.get((fb - 1, tb)))
                    if fb == 0:
                        w.append(t_ld[NCH - 1])
                for pi, (lh, rh) in enumerate([
                        (whl_hi(buf, d), xhi(d, tb * TB, TB)),
                        (whl_hi(buf, d), xlo(d, tb * TB, TB)),
                        (whl_lo(buf, d), xhi(d, tb * TB, TB))]):
                    tb_mm[tb] = mm(ps[tb][:, :], lh, rh,
                                   start=(d == 0 and pi == 0),
                                   stop=(d == NDT - 1 and pi == 2),
                                   waits=w if pi == 0 else [],
                                   inc=1 if (d == NDT - 1 and pi == 2) else 0)
        if fb == FQ + FZ - 1:
            ph1_mm = tb_mm[NTB - 1]
        t_wfree[buf] = tb_mm[NTB - 1]
        if is_q:
            tz = ve(lambda e: e.memset(convstage[:, 0:4], 0.0),
                    waits=(ph0_bar if fb == 0 else []), inc=1)
            tev = [None] * NTB
            for tb in range(NTB):
                tev[tb] = se(lambda e, tb=tb: e.copy(convstage[:, 4 + tb * TB: 4 + (tb + 1) * TB],
                                                     ps[tb][:, :]),
                             waits=[tb_mm[tb], tz], inc=1)
                t_evac[(fb, tb)] = tev[tb]
            cws = [convw[:, fb * 4 + j: fb * 4 + j + 1] for j in range(4)]
            tc0 = ve(lambda e, cw=cws[0]: e.tensor_scalar(gatebuf[:, :], convstage[:, 1:1 + T],
                                                          cw, None, ALU.mult, ALU.bypass),
                     waits=[tev[0], tev[1], tev[2], tev[3], t_spill_last], inc=1)
            tcj = tc0
            for j in range(1, 4):
                tcj = ve(lambda e, j=j, cw=cws[j]: e.scalar_tensor_tensor(
                    gatebuf[:, :], convstage[:, 1 + j:1 + j + T], cw, gatebuf[:, :],
                    ALU.mult, ALU.add),
                    waits=[tcj], inc=1)
            srcv = gatebuf[:, :].rearrange("p (c t) -> p c t", c=NCH)
            for cg in range(4):
                t_spill_last = dma_out(
                    qkvc_sp[cg * 4:(cg + 1) * 4, :, fb * 128:(fb + 1) * 128].rearrange("c p t -> p c t"),
                    srcv[:, cg * 4:(cg + 1) * 4, :], "qsp",
                    waits=[tcj])
        else:
            tev = [None] * NTB
            for tb in range(NTB):
                tsil = se(lambda e, tb=tb: e.activation(gatebuf[:, tb * TB:(tb + 1) * TB],
                                                        ps[tb][:, :], AF.Silu),
                          waits=[tb_mm[tb], t_gate_last, t_spill_last if zi == 0 else None],
                          inc=1)
                tev[tb] = ve(lambda e, tb=tb: e.tensor_scalar(gatebuf[:, tb * TB:(tb + 1) * TB],
                                                              gatebuf[:, tb * TB:(tb + 1) * TB],
                                                              normw_col[:, :], None,
                                                              ALU.mult, ALU.bypass),
                             waits=[tsil], inc=1)
                t_evac[(fb, tb)] = tsil
            srcv = gatebuf[:, :].rearrange("p (c t) -> p c t", c=NCH)
            for cg in range(4):
                t_gate_last = dma_out(
                    gate_sp[cg * 4:(cg + 1) * 4, :, zi * 128:(zi + 1) * 128].rearrange("c p t -> p c t"),
                    srcv[:, cg * 4:(cg + 1) * 4, :], "gsp",
                    waits=tev)

    ph1_bar = S.barrier_token()

    t_Sinit = ve(lambda e: e.memset(S_sb[:, :], 0.0), waits=ph1_bar, inc=1)
    t_out_dma = None
    t_chain_last_prev = {}
    t_prep_free_prev = {}
    NCH_RUN = 1 if only00 else NCH
    NK_RUN = 1 if only00 else NK
    DEFER_BND = True
    late_pending = [None]
    t_S_prev = {}
    pst = {}
    t_ep_prev = {}

    def make_boundary_thunks(c2, vt_tok):
        sm2 = smalls_p[c2 % 2]
        ld2 = ld_tm[:, c2 * 32:(c2 + 1) * 32]
        beta2 = beta_tm[:, c2 * 32:(c2 + 1) * 32]
        first = ph1_bar if c2 == 0 else []
        bn = {}
        th = []

        def b_dma():
            bn["chd"] = dma_in(ch_sb[:, :], qkvc_sp[c2, :, :], "ch",
                               waits=list(first) + vt_tok())
        th.append(b_dma)

        def b_tL():
            bn["tL"] = mm(ps[0][:, 0:32], tri_li[:, :], ld2, start=True, stop=True,
                          waits=[t_ld[c2], t_cst if c2 == 0 else None,
                                 pst.get("G"), pst.get("Kt")] + list(first), inc=1)
        th.append(b_tL)
        th.append(lambda: bn.__setitem__("tLe", ve(
            lambda e: e.tensor_copy(sm2["L_tm"][:, :], ps[0][:, 0:32]),
            waits=[bn["tL"]], inc=1)))
        th.append(lambda: bn.__setitem__("tLm", ve(
            lambda e: e.tensor_sub(sm2["Lm1_tm"][:, :], sm2["L_tm"][:, :], ld2),
            waits=[bn["tLe"]], inc=1)))
        th.append(lambda: bn.__setitem__("tgm", se(
            lambda e: e.activation(sm2["gm1_tm"][:, :], sm2["Lm1_tm"][:, :], AF.Exp),
            waits=[bn["tLm"]], inc=1)))
        th.append(lambda: bn.__setitem__("tngm", ve(
            lambda e: e.tensor_scalar_mul(sm2["neggm1_tm"][:, :], sm2["gm1_tm"][:, :], -1.0),
            waits=[bn["tgm"]], inc=1)))
        th.append(lambda: bn.__setitem__("tnb", ve(
            lambda e: e.tensor_scalar_mul(sm2["negb_ch"][:, :], beta2, -1.0),
            waits=[t_beta[c2]], inc=1)))
        th.append(lambda: bn.__setitem__("tLl", mm(
            ps[0][:, 32:64], ones_full[:, :], ld2, start=True, stop=True,
            waits=[bn["tLe"], t_ones3], inc=1)))
        th.append(lambda: bn.__setitem__("tcs1", ve(
            lambda e: e.tensor_sub(sm2["colscalb_tm"][:, :], ps[0][:, 32:64], sm2["L_tm"][:, :]),
            waits=[bn["tLl"], bn["tLe"]], inc=1)))
        th.append(lambda: bn.__setitem__("tcs2", se(
            lambda e: e.activation(sm2["colscalb_tm"][:, :], sm2["colscalb_tm"][:, :], AF.Exp),
            waits=[bn["tcs1"]], inc=1)))
        th.append(lambda: bn.__setitem__("tcs3", ve(
            lambda e: e.tensor_mul(sm2["colscalb_tm"][:, :], sm2["colscalb_tm"][:, :], beta2),
            waits=[bn["tcs2"]], inc=1)))
        th.append(lambda: bn.__setitem__("tgl1", ve(
            lambda e: e.tensor_copy(sm2["glast_bc"][:, :], ps[0][:, 32:64]),
            waits=[bn["tLl"], bn["tcs1"]], inc=1)))
        th.append(lambda: bn.__setitem__("tgl2", se(
            lambda e: e.activation(sm2["glast_bc"][:, :], sm2["glast_bc"][:, :], AF.Exp),
            waits=[bn["tgl1"]], inc=1)))
        for h in range(NK_RUN):
            kof = (NK + h) * 128
            qof = h * 128

            def b_sq1(h=h, kof=kof):
                bn["sq1"] = se(lambda e: e.activation(sqh[:, :], ch_sb[:, kof:kof + 128],
                                                      AF.Square),
                               waits=[bn["chd"], bn.get("m2")], inc=1)
            th.append(b_sq1)

            def b_m1(h=h):
                bn["m1"] = mm(ps[1][:1, 256:384], ones_col[:, :], sqh[:, :],
                              start=True, stop=True,
                              waits=[bn["sq1"], bn.get("ev"),
                                     pst.get("QG") if h == 0 else None]
                                    + (first if h == 0 else []),
                              inc=1)
            th.append(b_m1)

            def b_sq2(h=h, qof=qof):
                bn["sq2"] = se(lambda e: e.activation(sqh[:, :], ch_sb[:, qof:qof + 128],
                                                      AF.Square),
                               waits=[bn["m1"]], inc=1)
            th.append(b_sq2)

            def b_m2(h=h):
                bn["m2"] = mm(ps[1][:1, 384:512], ones_col[:, :], sqh[:, :],
                              start=True, stop=True, waits=[bn["sq2"]], inc=1)
            th.append(b_m2)

            def b_ev(h=h):
                bn["ev"] = ve(lambda e: e.tensor_copy(
                    rn_row[:1, h * 256:(h + 1) * 256], ps[1][:1, 256:512]),
                    waits=[bn["m2"],
                           pst.get("Kn") if h == 0 else None,
                           pst.get("Qn") if h == 0 else None], inc=1)
            th.append(b_ev)

        def b_fin():
            tcl = ve(lambda e: e.tensor_scalar(rn_row[:1, :], rn_row[:1, :], 1e-24, None,
                                               ALU.max, ALU.bypass),
                     waits=[bn["ev"]], inc=1)
            tra = ve(lambda e: e.reciprocal(sq_all[0:1, 0:2048], rn_row[:1, 0:2048]),
                     waits=[tcl], inc=1)
            tsa = se(lambda e: e.activation(rn_row[:1, 0:2048], sq_all[0:1, 0:2048],
                                            AF.Sqrt),
                     waits=[tra], inc=1)
            trb = ve(lambda e: e.reciprocal(sq_all[0:1, 0:2048], rn_row[:1, 2048:4096]),
                     waits=[tsa], inc=1)
            bn["trs"] = se(lambda e: e.activation(rn_row[:1, 2048:4096],
                                                  sq_all[0:1, 0:2048], AF.Sqrt),
                           waits=[trb], inc=1)
            pst["G"] = bn["tcs1"]
            pst["Kt"] = bn["tgl1"]
            pst["QG"] = bn["ev"]
            pst.pop("Kn", None)
            pst.pop("Qn", None)
        th.append(b_fin)
        n_early = 13
        return bn, th[1:n_early] , [th[0]] + th[n_early:]

    bnd_next = None
    for c in range(NCH_RUN):
        Sin_t = S_sb
        Sout_t = S_sb
        csl = (c * 32, (c + 1) * 32)
        ld_ch = ld_tm[:, csl[0]:csl[1]]
        beta_ch = beta_tm[:, csl[0]:csl[1]]
        sm = smalls_p[c % 2]
        if c == 0 or not DEFER_BND:
            bnd, bth_e, bth_l = make_boundary_thunks(
                c, (lambda: []) if c == 0
                else (lambda vw=[t_S_prev[(NK_RUN - 1, 0)], t_S_prev[(NK_RUN - 1, 1)]]: vw))
            for f_ in bth_e + bth_l:
                f_()
        else:
            bnd = bnd_next
        t_chd = bnd["chd"]
        tLe = bnd["tLe"]
        tngm = bnd["tngm"]
        tnb = bnd["tnb"]
        tcs3 = bnd["tcs3"]
        tgl2 = bnd["tgl2"]
        trs = bnd["trs"]
        t_gd = dma_in(gate_ch[:, :], gate_sp[c, :, :], "gch",
                      waits=(list(ph1_bar) if c == 0
                             else [t_ep_prev[hp] for hp in range(2 * NK_RUN)]))
        if only00:
            tnb = ve(lambda e, sm=sm: e.memset(sm["ssq_tm"][:, :], 1.0), waits=[tnb], inc=1)
            tnb = ve(lambda e: e.memset(outT_ch[:, :], 0.0), waits=[tnb], inc=1)

        t_S_upd = {}
        t_O_ev = {}
        t_chain_last = dict(t_chain_last_prev)
        t_prep_free = dict(t_prep_free_prev)
        npairs = (NK_RUN + 1) // 2

        def make_prep_thunks(pair):
            hs_ = [h_ for h_ in (2 * pair, 2 * pair + 1) if h_ < NK_RUN]
            pp_ = pair % 2
            ptoks = [dict() for _ in hs_]
            thunks = []
            for hi, h in enumerate(hs_):
                T_ = HT[pp_][hi]
                kof = (NK + h) * 128
                qof = h * 128
                free_w = [t_prep_free.get((pp_, hi))]
                pt = ptoks[hi]
                prevp = dict(pst)

                def th_mk(T_=T_, h=h, pt=pt, free_w=free_w, prevp=prevp):
                    pt["_mk"] = mm(ps[0][:, 0:128], ones_row[:, :],
                                   rn_row[:1, h * 256: h * 256 + 128],
                                   start=True, stop=True,
                                   waits=[trs, pst.get("G"), pst.get("Kt")] + free_w,
                                   inc=1)

                def th_Kn(T_=T_, kof=kof, pt=pt, free_w=free_w):
                    pt["Kn"] = ve(lambda e: e.tensor_mul(
                        T_["Kn"][:, :], ch_sb[:, kof:kof + 128], ps[0][:, 0:128]),
                        waits=[pt["_mk"]] + free_w, inc=1)

                def th_mq(T_=T_, h=h, pt=pt):
                    pt["_mq"] = mm(ps[1][:, 0:128], ones_row[:, :],
                                   rn_row[:1, h * 256 + 128: h * 256 + 256],
                                   start=True, stop=True,
                                   waits=[trs, pt["Kn"], pst.get("QG")], inc=1)

                def th_Qn(T_=T_, qof=qof, pt=pt):
                    pt["Qn"] = ve(lambda e: e.tensor_mul(
                        T_["Qn"][:, :], ch_sb[:, qof:qof + 128], ps[1][:, 0:128]),
                        waits=[pt["_mq"]], inc=1)

                def th_mg(T_=T_, pt=pt):
                    pt["_mg"] = mm(ps[0][:, 0:128], T_["Kn"][:, :], T_["Kn"][:, :],
                                   start=True, stop=True, waits=[pt["Kn"]], inc=1)

                def th_Ge(T_=T_, pt=pt, free_w=free_w):
                    pt["G"] = se(lambda e: e.copy(T_["G"][:, :], ps[0][:, 0:128]),
                                 waits=[pt["_mg"]] + free_w, inc=1)

                def th_mqg(T_=T_, pt=pt):
                    pt["_mqg"] = mm(ps[1][:, 0:128], T_["Kn"][:, :], T_["Qn"][:, :],
                                    start=True, stop=True, waits=[pt["Qn"], pt["G"]], inc=1)

                def th_QGe(T_=T_, pt=pt):
                    pt["QG"] = se(lambda e: e.copy(T_["QG"][:, :], ps[1][:, 0:128]),
                                  waits=[pt["_mqg"]], inc=1)

                def th_mkt(T_=T_, pt=pt):
                    pt["_mkt"] = transp(ps[0][:, 128:256], T_["Kn"][:, :],
                                        waits=[pt["G"], t_cst], inc=1)

                def th_Kte(T_=T_, pt=pt):
                    pt["Kt"] = se(lambda e: e.copy(T_["Kt"][:, :], ps[0][:, 128:256]),
                                  waits=[pt["_mkt"]], inc=1)
                    pst.clear()
                    pst.update(pt)

                thunks += [th_mk, th_Kn, th_mq, th_Qn, th_mg, th_Ge, th_mqg, th_QGe,
                           th_mkt, th_Kte]
            return ptoks, thunks

        prep_pending = None
        for pair in range(npairs):
            hs = [h_ for h_ in (2 * pair, 2 * pair + 1) if h_ < NK_RUN]
            pp = pair % 2
            if pair == 0:
                ptoks0, thunks0 = make_prep_thunks(0)
                for th in thunks0:
                    th()
                prep_toks = ptoks0
            else:
                prep_toks = prep_pending
            bmode = False
            if pair + 1 < npairs:
                prep_pending, next_thunks = make_prep_thunks(pair + 1)
            else:
                next_thunks = []
                if DEFER_BND and c + 1 < NCH_RUN:
                    bnd_next, next_thunks, late_thunks = make_boundary_thunks(
                        c + 1,
                        lambda: [t_S_upd[(h_, p2)] for h_ in hs for p2 in range(2)])
                    bmode = True
            nthi = 0
            prep_tok = {hi: prep_toks[hi] for hi in range(len(hs))}
            chains = [(hi, h, p_) for hi, h in enumerate(hs) for p_ in range(2)]
            tok = {}
            NLEV = 7
            NST = 12 + 6 * NLEV + 14
            for st in range(NST):
                if bmode:
                    nemit = 0 if st < 9 else (3 if st == 9 else (1 if st < 40 else 4))
                    while nemit > 0 and nthi < len(next_thunks):
                        next_thunks[nthi]()
                        nthi += 1
                        nemit -= 1
                elif st % 3 == 0 and nthi < len(next_thunks):
                    next_thunks[nthi]()
                    nthi += 1
                for ci, (hi, h, p_) in enumerate(chains):
                    hp = 2 * h + p_
                    d = CT[ci]
                    T_ = HT[pp][hi]
                    pk = prep_tok[hi]
                    B_ = ps[2 + ci]
                    R0 = B_[:, 0:128]
                    R1 = B_[:, 128:256]
                    Ssl = Sin_t[:, hp * 128:(hp + 1) * 128]
                    Sosl = Sout_t[:, hp * 128:(hp + 1) * 128]
                    Lcol = sm["L_tm"][:, hp:hp + 1]
                    vof = (2 * NK + hp) * 128
                    k = ci
                    reuse = [t_chain_last.get(ci)]
                    if st == 0:
                        tok[(k, "trild")] = ve(lambda e, d=d, hp=hp, ld_ch=ld_ch: e.tensor_scalar(
                            d["Tmp"][:, :], tri_lu[:, :], ld_ch[:, hp:hp + 1], None,
                            ALU.mult, ALU.bypass),
                            waits=[t_ld[c], t_cst, tLe] + reuse, inc=1)
                    elif st == 1:
                        tok[(k, "lm1bc")] = mm(R0, ones_full[:, :], d["Tmp"][:, :],
                                               start=True, stop=True,
                                               waits=[tok[(k, "trild")]] + reuse, inc=1)
                    elif st == 2:
                        tok[(k, "argD")] = ve(lambda e, d=d, R0=R0, Lcol=Lcol: e.tensor_scalar(
                            d["ArgMt"][:, :], R0, Lcol, 0.0, ALU.subtract, ALU.min),
                            waits=[tok[(k, "lm1bc")]] + reuse, inc=1)
                    elif st == 3:
                        tok[(k, "expD")] = se(lambda e, d=d: e.activation(
                            d["Expb"][:, :], d["ArgMt"][:, :], AF.Exp),
                            waits=[tok[(k, "argD")], trs] + reuse, inc=1)
                    elif st == 4:
                        tok[(k, "et1")] = ve(lambda e, d=d, T_=T_: e.tensor_mul(
                            d["Tmp"][:, :], d["Expb"][:, :], T_["G"][:, :]),
                            waits=[tok[(k, "expD")], pk["G"]], inc=1)
                    elif st == 5:
                        tok[(k, "Et0")] = ve(lambda e, d=d, hp=hp, sm=sm: e.scalar_tensor_tensor(
                            d["Et_a"][:, :], d["Tmp"][:, :], sm["negb_ch"][:, hp:hp + 1],
                            tri_lu[:, :], ALU.mult, ALU.mult),
                            waits=[tok[(k, "et1")], tnb], inc=1)
                    elif st == 6:
                        tok[(k, "EtT")] = transp(R0, d["Et_a"][:, :],
                                                 waits=[tok[(k, "Et0")], tok[(k, "argD")]], inc=1)
                    elif st == 7:
                        tok[(k, "Etm0")] = se(lambda e, d=d, R0=R0: e.copy(d["Etm_a"][:, :], R0),
                                              waits=[tok[(k, "EtT")]], inc=1)
                    elif st == 8:
                        tok[(k, "VtT")] = transp(R1, ch_sb[:, vof:vof + 128],
                                                 waits=[t_chd, tok[(k, "Etm0")]] + reuse, inc=1)
                    elif st == 9:
                        tok[(k, "Vte")] = se(lambda e, d=d, R1=R1: e.copy(d["VtWsc"][:, :], R1),
                                             waits=[tok[(k, "VtT")]], inc=1)
                    elif st == 10:
                        tok[(k, "KS")] = mm(R0, T_["Kn"][:, :], Ssl, start=True, stop=True,
                                            waits=[pk["Kn"], tok[(k, "Etm0")],
                                                   t_Sinit if c == 0 else None,
                                                   t_S_prev.get((h, p_))], inc=1)
                    elif st == 11:
                        tok[(k, "B")] = ve(lambda e, d=d, R0=R0, hp=hp, sm=sm: e.scalar_tensor_tensor(
                            d["Y_a"][:, :], R0, sm["neggm1_tm"][:, hp:hp + 1],
                            d["VtWsc"][:, :], ALU.mult, ALU.add),
                            waits=[tok[(k, "KS")], tok[(k, "Vte")], tngm], inc=1)
                    elif 12 <= st < 12 + 6 * NLEV:
                        lev, sub = divmod(st - 12, 6)
                        Ya = d["Y_a"] if lev % 2 == 0 else d["Y_b"]
                        Yb = d["Y_b"] if lev % 2 == 0 else d["Y_a"]
                        Eta = d["Et_a"] if lev % 2 == 0 else d["Et_b"]
                        Etb = d["Et_b"] if lev % 2 == 0 else d["Et_a"]
                        Etma = d["Etm_a"] if lev % 2 == 0 else d["Etm_b"]
                        Etmb = d["Etm_b"] if lev % 2 == 0 else d["Etm_a"]
                        if sub == 0:
                            prevY = tok[(k, "B")] if lev == 0 else tok[(k, f"add{lev-1}")]
                            prevE = tok[(k, "Etm0")] if lev == 0 else tok[(k, f"tre{lev-1}")]
                            tok[(k, f"ap{lev}")] = mm(R0, Eta[:, :], Ya[:, :],
                                                      start=True, stop=True,
                                                      waits=[prevY, prevE], inc=1)
                        elif sub == 1:
                            tok[(k, f"add{lev}")] = ve(lambda e, Ya=Ya, Yb=Yb, R0=R0:
                                                       e.tensor_add(Yb[:, :], R0, Ya[:, :]),
                                                       waits=[tok[(k, f"ap{lev}")]], inc=1)
                        elif lev < 6:
                            if sub == 2:
                                tok[(k, f"sq{lev}")] = mm(R1, Etma[:, :], Eta[:, :],
                                                          start=True, stop=True,
                                                          waits=[tok[(k, f"add{lev}")],
                                                                 tok.get((k, f"tre{lev-1}")),
                                                                 tok[(k, "Vte")] if lev == 0 else None],
                                                          inc=1)
                            elif sub == 3:
                                tok[(k, f"sqe{lev}")] = se(lambda e, Etb=Etb, R1=R1:
                                                           e.copy(Etb[:, :], R1),
                                                           waits=[tok[(k, f"sq{lev}")]], inc=1)
                            elif sub == 4:
                                tok[(k, f"tr{lev}")] = transp(R0, Etb[:, :],
                                                              waits=[tok[(k, f"sqe{lev}")],
                                                                     tok[(k, f"add{lev}")]], inc=1)
                            elif sub == 5:
                                tok[(k, f"tre{lev}")] = se(lambda e, Etmb=Etmb, R0=R0:
                                                           e.copy(Etmb[:, :], R0),
                                                           waits=[tok[(k, f"tr{lev}")]], inc=1)
                    else:
                        fst = st - 12 - 6 * NLEV
                        Wbuf = d["Y_b"] if (NLEV - 1) % 2 == 0 else d["Y_a"]
                        tokW = tok[(k, f"add{NLEV-1}")]
                        if fst == 0:
                            tok[(k, "trild2")] = ve(lambda e, d=d, hp=hp, ld_ch=ld_ch: e.tensor_scalar(
                                d["Tmp"][:, :], tri_li[:, :], ld_ch[:, hp:hp + 1], None,
                                ALU.mult, ALU.bypass),
                                waits=[tokW], inc=1)
                        elif fst == 1:
                            tok[(k, "lbc")] = mm(R0, ones_full[:, :], d["Tmp"][:, :],
                                                 start=True, stop=True,
                                                 waits=[tok[(k, "trild2")],
                                                        tok[(k, f"tre{NLEV-2}")]], inc=1)
                        elif fst == 2:
                            tok[(k, "argM")] = ve(lambda e, d=d, R0=R0, Lcol=Lcol: e.tensor_scalar(
                                d["ArgMt"][:, :], R0, Lcol, 0.0, ALU.subtract, ALU.min),
                                waits=[tok[(k, "lbc")], tok[(k, "expD")]], inc=1)
                        elif fst == 3:
                            tok[(k, "expM")] = se(lambda e, d=d: e.activation(
                                d["Expb"][:, :], d["ArgMt"][:, :], AF.Exp),
                                waits=[tok[(k, "argM")]], inc=1)
                        elif fst == 4:
                            tok[(k, "mt1")] = ve(lambda e, d=d, T_=T_: e.tensor_mul(
                                d["Tmp"][:, :], d["Expb"][:, :], T_["QG"][:, :]),
                                waits=[tok[(k, "expM")], pk["QG"], tok[(k, "lbc")]], inc=1)
                        elif fst == 5:
                            tok[(k, "Mt")] = ve(lambda e, d=d, hp=hp, beta_ch=beta_ch:
                                                e.scalar_tensor_tensor(
                                d["ArgMt"][:, :], d["Tmp"][:, :], beta_ch[:, hp:hp + 1],
                                tri_li[:, :], ALU.mult, ALU.mult),
                                waits=[tok[(k, "mt1")], tok[(k, "expM")]], inc=1)
                        elif fst == 6:
                            tok[(k, "gbc")] = se(lambda e, d=d, R0=R0: e.activation(
                                d["Expb"][:, :], R0, AF.Exp),
                                waits=[tok[(k, "lbc")], tok[(k, "mt1")]], inc=1)
                        elif fst == 7:
                            tok[(k, "Qg")] = ve(lambda e, d=d, T_=T_: e.tensor_mul(
                                d["Tmp"][:, :], T_["Qn"][:, :], d["Expb"][:, :]),
                                waits=[tok[(k, "gbc")], pk["Qn"], tok[(k, "Mt")]], inc=1)
                        elif fst == 8:
                            o1 = mm(R1, d["Tmp"][:, :], Ssl, start=True, stop=False,
                                    waits=[tok[(k, "Qg")], tok[(k, "KS")],
                                           tok[(k, f"sqe{NLEV-2}")],
                                           t_S_prev.get((h, p_))], inc=0)
                            tok[(k, "Omm")] = mm(R1, d["ArgMt"][:, :], Wbuf[:, :],
                                                 start=False, stop=True,
                                                 waits=[], inc=1)
                        elif fst == 9:
                            t_O_ev[(h, p_)] = se(lambda e, R1=R1, hp=hp: e.copy(
                                O_all[:, hp * 128:(hp + 1) * 128], R1),
                                waits=[tok[(k, "Omm")]], inc=1)
                        elif fst == 10:
                            tsq_ = ve(lambda e, d=d, hp=hp: e.tensor_mul(
                                d["Expb"][:, :], O_all[:, hp * 128:(hp + 1) * 128],
                                O_all[:, hp * 128:(hp + 1) * 128]),
                                waits=[t_O_ev[(h, p_)]], inc=1)
                            tok[(k, "ssq")] = ve(lambda e, d=d, hp=hp, sm=sm: e.reduce_sum(
                                sm["ssq_tm"][:, hp:hp + 1], d["Expb"][:, :], AX.X),
                                waits=[tsq_], inc=1)
                        elif fst == 11:
                            tok[(k, "Wsc")] = ve(lambda e, d=d, hp=hp, Wbuf=Wbuf, sm=sm: e.tensor_scalar(
                                d["VtWsc"][:, :], Wbuf[:, :], sm["colscalb_tm"][:, hp:hp + 1],
                                None, ALU.mult, ALU.bypass),
                                waits=[tokW, tcs3, tok[(k, "B")]], inc=1)
                        elif fst == 12:
                            tok[(k, "msu")] = mm(R0, T_["Kt"][:, :], d["VtWsc"][:, :],
                                                 start=True, stop=True,
                                                 waits=[pk["Kt"], tok[(k, "Wsc")],
                                                        t_O_ev[(h, p_)], tok[(k, "gbc")]], inc=1)
                        elif fst == 13:
                            t_S_upd[(h, p_)] = ve(
                                lambda e, R0=R0, Ssl=Ssl, Sosl=Sosl, hp=hp, sm=sm:
                                e.scalar_tensor_tensor(Sosl, Ssl, sm["glast_bc"][:, hp:hp + 1],
                                                       R0, ALU.mult, ALU.add),
                                waits=[tok[(k, "msu")], tgl2], inc=1)
                            t_chain_last[ci] = t_S_upd[(h, p_)]
            while nthi < len(next_thunks):
                next_thunks[nthi]()
                nthi += 1
            if bmode:
                for f_ in late_thunks:
                    f_()
            for hi, h in enumerate(hs):
                t_prep_free[(pp, hi)] = t_S_upd[(h, 1)]
        t_chain_last_prev = t_chain_last
        t_prep_free_prev = t_prep_free
        trmA = ve(lambda e, sm=sm: e.tensor_scalar(sm["rms_tm"][:, :], sm["ssq_tm"][:, :],
                                            1.0 / DV, 1e-6, ALU.mult, ALU.add),
                  waits=[t_S_upd[(NK_RUN - 1, 0)], t_S_upd[(NK_RUN - 1, 1)],
                         t_O_ev[(NK_RUN - 1, 0)], t_O_ev[(NK_RUN - 1, 1)]],
                  inc=1)
        trm0 = ve(lambda e, sm=sm: e.reciprocal(sm["ssq_tm"][:, :], sm["rms_tm"][:, :]),
                  waits=[trmA], inc=1)
        trm = se(lambda e, sm=sm: e.activation(sm["rms_tm"][:, :], sm["ssq_tm"][:, :], AF.Sqrt),
                 waits=[trm0], inc=1)
        t_ep = {}
        for hp in range(2 * NK_RUN):
            bE = ps[6 + hp % 2]
            osl = O_all[:, hp * 128:(hp + 1) * 128]
            t1 = ve(lambda e, osl=osl, hp=hp, sm=sm: e.tensor_scalar(
                CT[hp % 4]["Tmp"][:, :], osl, sm["rms_tm"][:, hp:hp + 1], None,
                ALU.mult, ALU.bypass),
                waits=[trm, t_O_ev[(hp // 2, hp % 2)], t_ep.get(hp - 4),
                       t_chain_last.get(hp % 4)], inc=1)
            t2 = transp(bE[:, 0:128], CT[hp % 4]["Tmp"][:, :],
                        waits=[t1, t_ep.get(hp - 2)], inc=1)
            ep1 = ve(lambda e, bE=bE, hp=hp: e.tensor_mul(
                CT[hp % 4]["Et_a"][:, 0:128], bE[:, 0:128],
                gate_ch[:, hp * 128:(hp + 1) * 128]),
                waits=[t2, t_gd], inc=1)
            eph = ve(lambda e, hp=hp: e.tensor_copy(
                outT_ch_hl[:, hp * 256:hp * 256 + 128], CT[hp % 4]["Et_a"][:, 0:128]),
                waits=[ep1, t_out_dma], inc=1)
            t_ep[hp] = ve(lambda e, hp=hp: e.tensor_sub(
                outT_ch_hl[:, hp * 256 + 128:hp * 256 + 256],
                CT[hp % 4]["Et_a"][:, 0:128],
                outT_ch_hl[:, hp * 256:hp * 256 + 128]),
                waits=[eph], inc=1)
        if debug and c == 0:
            dma_out(dbg[:, 2176:2304], Sout_t[:, 0:128], "dbgd", waits=[t_S_upd[(0, 0)]])
            dma_out(dbg[:, 2304:2432], Sout_t[:, 128:256], "dbgd", waits=[t_S_upd[(0, 1)]])
        t_out_dma = dma_out(outT_sp[c, :, :], outT_ch_hl[:, :], "osp",
                            waits=[t_ep[hp] for hp in range(2 * NK_RUN)])
        t_ep_prev = t_ep
        t_S_prev = dict(t_S_upd)

    ph2_bar = S.barrier_token()

    if not only00:
        t_q2 = [None] * NDT
        t_i2d = [None, None]
        for ki in range(NDT):
            buf = ki % 2
            t_i2d[buf] = dma_in(in2buf[buf][:, :], in2[ki * 128:(ki + 1) * 128, :],
                                f"i2{buf}",
                                waits=[t_q2[ki - 2]] + (ph2_bar if ki < 2 else []))
            t_q2[ki] = ve(lambda e, ki=ki, buf=buf: e.tensor_scalar(
                q28[:, ki * HID:(ki + 1) * HID], in2buf[buf][:, :], scale1_col[:, :], None,
                ALU.mult, ALU.bypass),
                waits=[t_i2d[buf]], inc=1)

        t_y_st = []
        t_wc = None
        t_ot = [None, None]
        t_ot_free = [None, None]
        t_xd = [None, None]
        t_ybufree = {}
        for cb in range(4):
            t_wcs = {}
            for ki in range(32):
                jb = ki % 2
                t_wcd = dma_in(wout_in[jb][:, :], wout_l[cb, ki, :, :], f"wc{jb}",
                               waits=[t_wcs.get(ki - 2)]
                                     + (ph2_bar if cb == 0 and ki < 2 else []))
                twh2 = ve(lambda e, ki=ki, jb=jb: e.tensor_copy(
                    woutc_hl[:, ki * 512:(ki + 1) * 512], wout_in[jb][:, :]),
                    waits=[t_wcd, t_ot_free[0], t_ot_free[1]], inc=1)
                t_wcs[ki] = ve(lambda e, ki=ki, jb=jb: e.tensor_sub(
                    woutc_hl[:, 16384 + ki * 512:16384 + (ki + 1) * 512],
                    wout_in[jb][:, :], woutc_hl[:, ki * 512:(ki + 1) * 512]),
                    waits=[twh2], inc=1)
            t_wc_done = t_wcs[31]
            for ti in range(NCH):
                buf = ti % 2
                t_ot[buf] = dma_in(outT_t[buf][:, :], outT_sp[ti, :, :], f"ot{buf}",
                                   waits=[t_ot_free[buf]] + (ph2_bar if cb == 0 and ti < 2 else []))
                t_xd[buf] = dma_in(xbuf[buf][:, :], x_nat[ti * 128:(ti + 1) * 128, cb * 512:(cb + 1) * 512],
                                   f"xb{buf}",
                                   waits=[t_ybufree.get(16 * cb + ti - 2)])
                pw = ps[buf]
                pf = ps[2 + buf]
                mmw = None
                w0 = ([t_ot[buf], t_wc_done, t_ybufree.get(16 * cb + ti - 2)]
                      + (ph2_bar if cb == 0 and ti == 0 else []))
                for ki in range(32):
                    for pi, (lh, rh) in enumerate([
                            (outT_t[buf][:, ki * 256:ki * 256 + 128],
                             woutc_hl[:, ki * 512:(ki + 1) * 512]),
                            (outT_t[buf][:, ki * 256 + 128:ki * 256 + 256],
                             woutc_hl[:, ki * 512:(ki + 1) * 512]),
                            (outT_t[buf][:, ki * 256:ki * 256 + 128],
                             woutc_hl[:, 16384 + ki * 512:16384 + (ki + 1) * 512])]):
                        mmw = mm(pw[:, :], lh, rh,
                                 start=(ki == 0 and pi == 0), stop=(ki == 31 and pi == 2),
                                 waits=w0 if (ki == 0 and pi == 0) else [],
                                 inc=1 if (ki == 31 and pi == 2) else 0)
                t_ot_free[buf] = mmw
                mmf = None
                for ki in range(NDT):
                    mmf = mm(pf[:, :], xn8[:, ki * T + ti * 128: ki * T + (ti + 1) * 128],
                             q28[:, ki * HID + cb * 512: ki * HID + (cb + 1) * 512],
                             start=(ki == 0), stop=(ki == NDT - 1),
                             waits=[t_q2[NDT - 1], t_xn8[NDT - 1], mmw] if ki == 0 else [],
                             inc=1 if ki == NDT - 1 else 0)
                tf1 = ve(lambda e, pf=pf, buf=buf: e.tensor_scalar(
                    fbuf[buf][:, :], pf[:, :], rscale_col[:, :], None, ALU.mult, ALU.bypass),
                    waits=[mmf, t_rs], inc=1)
                tf2 = ve(lambda e, buf=buf: e.tensor_add(fbuf[buf][:, :], fbuf[buf][:, :],
                                                         xbuf[buf][:, :]),
                         waits=[tf1, t_xd[buf]], inc=1)
                ty = ve(lambda e, pw=pw, buf=buf: e.tensor_add(ybuf[buf][:, :], pw[:, :],
                                                               fbuf[buf][:, :]),
                        waits=[tf2, mmw], inc=1)
                tst = dma_out(y_out[ti * 128:(ti + 1) * 128, cb * 512:(cb + 1) * 512],
                              ybuf[buf][:, :], f"y{buf}", waits=[ty])
                t_ybufree[16 * cb + ti] = tst
                t_y_st.append(tst)
            t_wc = t_wc_done

        final_waits = t_y_st[-4:]


    for sl in S.slots:
        sems[sl] = es.enter_context(nc.semaphore(f"sem_{sl}"))
    blk = es.enter_context(nc.Block())

    @blk.sync
    def _(eng):
        S.replay("sync", eng, sems, self_waits=self_waits)

    @blk.gpsimd
    def _(eng):
        S.replay("gpsimd", eng, sems, self_waits=self_waits)
        for sl, cnt in S.slots.items():
            if S.slot_eng.get(sl) == "gpsimd":
                eng.wait_ge(sems[sl], cnt)

    @blk.tensor
    def _(eng):
        S.replay("tensor", eng, sems, self_waits=self_waits)

    @blk.vector
    def _(eng):
        S.replay("vector", eng, sems, self_waits=self_waits)

    @blk.scalar
    def _(eng):
        S.replay("scalar", eng, sems, self_waits=self_waits)

    es.close()
    return nc


def host_prep(inputs):
    f = np.float32
    Wqkv = np.asarray(inputs["L0_attn_W_qkv"], f)
    Wz = np.asarray(inputs["L0_attn_W_z"], f)
    Wb = np.asarray(inputs["L0_attn_W_b"], f)
    Wa = np.asarray(inputs["L0_attn_W_a"], f)
    Wout = np.asarray(inputs["L0_attn_W_out"], f)
    conv_w = np.asarray(inputs["L0_attn_conv_w"], f)
    common = {}
    wq = Wqkv.reshape(FQ, 128, NDT, 128).transpose(0, 3, 2, 1).reshape(FQ, 128, HID)
    common["wqkv_l"] = np.ascontiguousarray(wq)
    wzt = Wz.reshape(FZ, 128, NDT, 128).transpose(0, 3, 2, 1).reshape(FZ, 128, HID)
    common["wz_l"] = np.ascontiguousarray(wzt)
    Wab = np.concatenate([Wb, Wa], 0)
    common["wab_l"] = np.ascontiguousarray(Wab.reshape(64, NDT, 128).transpose(2, 1, 0).reshape(128, NDT * 64))
    wo = Wout.reshape(4, 512, 32, 128).transpose(0, 2, 3, 1)
    common["wout_l"] = np.ascontiguousarray(wo)
    common["in2"] = np.ascontiguousarray(np.asarray(inputs["L0_ffn_input2"], f))
    common["convw_in"] = np.ascontiguousarray(conv_w.reshape(FQ, 128, 4).transpose(1, 0, 2).reshape(128, FQ * 4))
    common["dtbias_bc"] = np.ascontiguousarray(np.tile(np.asarray(inputs["L0_attn_dt_bias"], f)[None, :], (128, 1)))
    common["alog_bc"] = np.ascontiguousarray(np.tile(np.asarray(inputs["L0_attn_A_log"], f)[None, :], (128, 1)))
    common["normw_bc"] = np.ascontiguousarray(np.tile(np.asarray(inputs["L0_attn_norm_w"], f)[None, :], (128, 1)))
    common["normw_col"] = np.ascontiguousarray(np.asarray(inputs["L0_attn_norm_w"], f)[:, None])
    common["scale_col"] = np.full((128, 1), np.asarray(inputs["L0_ffn_scale"], f), f)
    common["scale1_col"] = np.full((128, 1), np.asarray(inputs["L0_ffn_scale1"], f), f)
    a = np.arange(128)
    common["tri_lu"] = (a[:, None] < a[None, :]).astype(f)
    common["tri_li"] = (a[:, None] <= a[None, :]).astype(f)
    common["ident"] = np.eye(128, dtype=f)
    x = np.asarray(inputs["L0_attn_x"], f)
    maps = []
    for b in range(8):
        m = dict(common)
        m["x_nat"] = np.ascontiguousarray(x[b])
        m["xT"] = np.ascontiguousarray(x[b].T)
        maps.append(m)
    return maps


_NC_CACHE = {}

def kernel(**inputs):
    maps = host_prep(inputs)
    if "nc" not in _NC_CACHE:
        _NC_CACHE["nc"] = build_nc(debug=False)
    nc = _NC_CACHE["nc"]
    res = run_bass_kernel_spmd(nc, maps, core_ids=list(range(8)))
    y = np.stack([np.asarray(res.results[b]["y"]) for b in range(8)], 0)
    return y.astype(np.float32)


# version 10 (final)
# speedup vs baseline: 1.0582x; 1.0582x over previous
import numpy as np
from contextlib import ExitStack
import concourse.bass as bass
import concourse.mybir as mybir
from concourse.bass_utils import run_bass_kernel_spmd

F32 = mybir.dt.float32
BF16 = mybir.dt.bfloat16
F8 = mybir.dt.float8e4
AF = mybir.ActivationFunctionType
ALU = mybir.AluOpType
AX = mybir.AxisListType

T = 2048
HID = 2048
C = 128
NCH = T // C
NK = 16
P2 = 2
NV = 32
DK = 128
DV = 128
NDT = HID // 128
FQ = 64
FZ = 32
NTB = 4
TB = 512

ENGINES = ("sync", "gpsimd", "tensor", "vector", "scalar")


class Sched:

    def __init__(self):
        self.ops = {e: [] for e in ENGINES}
        self.done = {e: 0 for e in ENGINES}
        self.slots = {}

    def emit(self, eng, fn, waits=(), inc=0, slot=None):
        ws = [w for w in waits if w is not None]
        self.ops[eng].append((fn, ws, inc, slot))
        if slot is not None:
            self.slots[slot] = self.slots.get(slot, 0) + 16
            self.slot_eng = getattr(self, "slot_eng", {})
            self.slot_eng[slot] = eng
            return (slot, self.slots[slot])
        if inc:
            self.done[eng] += inc
            return (eng, self.done[eng])
        return None

    def barrier_token(self):
        toks = [(e, self.done[e]) for e in ENGINES if self.done[e] > 0]
        toks += [(sl, v) for sl, v in self.slots.items()]
        return toks

    SELF_WAIT_ENGINES = ("scalar", "gpsimd", "vector")

    def replay(self, engname, engine, sems, self_waits=False):
        hw = {}
        own = -1
        for fn, waits, inc, slot in self.ops[engname]:
            for peng, val in waits:
                if peng == engname:
                    if (self_waits or engname in self.SELF_WAIT_ENGINES) and val > own:
                        engine.wait_ge(sems[peng], val)
                        own = val
                    continue
                if hw.get(peng, 0) >= val:
                    continue
                engine.wait_ge(sems[peng], val)
                hw[peng] = val
            instr = fn(engine)
            if slot is not None:
                instr.then_inc(sems[slot], 16)
            elif inc:
                instr.then_inc(sems[engname], inc)


def build_nc(debug=False, detect_races=False, self_waits=False, only00=False):
    nc = bass.Bass("TRN2", target_bir_lowering=False, detect_race_conditions=detect_races)

    di = {}
    def inp(name, shape, dtype=F32):
        di[name] = nc.dram_tensor(name, shape, dtype, kind="ExternalInput").ap()
        return di[name]

    x_nat = inp("x_nat", [T, HID])
    xT = inp("xT", [HID, T])
    wqkv_l = inp("wqkv_l", [FQ, 128, HID])
    wz_l = inp("wz_l", [FZ, 128, HID])
    wab_l = inp("wab_l", [128, NDT * 64])
    wout_l = inp("wout_l", [4, 32, 128, 512])
    in2 = inp("in2", [HID, HID])
    convw_in = inp("convw_in", [128, FQ * 4])
    dtbias_bc_in = inp("dtbias_bc", [128, 32])
    alog_bc_in = inp("alog_bc", [128, 32])
    normw_bc_in = inp("normw_bc", [128, 128])
    normw_col_in = inp("normw_col", [128, 1])
    scale_col_in = inp("scale_col", [128, 1])
    scale1_col_in = inp("scale1_col", [128, 1])
    tri_lu_in = inp("tri_lu", [128, 128])
    tri_li_in = inp("tri_li", [128, 128])
    ident_in = inp("ident", [128, 128])

    y_out = nc.dram_tensor("y", [T, HID], F32, kind="ExternalOutput").ap()

    sk = "ExternalOutput" if debug else "Internal"
    qkvc_sp = nc.dram_tensor("qkvc_sp", [NCH, 128, FQ * 128], F32, kind=sk).ap()
    gate_sp = nc.dram_tensor("gate_sp", [NCH, 128, FZ * 128], F32, kind=sk).ap()
    outT_sp = nc.dram_tensor("outT_sp", [NCH, 128, 2 * FZ * 128], BF16, kind=sk).ap()
    dbg = None
    if debug:
        dbg = nc.dram_tensor("dbg", [128, 4096], F32, kind="ExternalOutput").ap()
        ssq_dbg = nc.dram_tensor("ssq_dbg", [1, T], F32, kind="ExternalOutput").ap()
        var_dbg = nc.dram_tensor("var_dbg", [1, T], F32, kind="ExternalOutput").ap()
        rec_dbg = nc.dram_tensor("rec_dbg", [1, T], F32, kind="ExternalOutput").ap()
        musq_dbg = nc.dram_tensor("musq_dbg", [1, T], F32, kind="ExternalOutput").ap()
        xnT_dbg = nc.dram_tensor("xnT_dbg", [128, NDT * T], F32, kind="ExternalOutput").ap()

    es = ExitStack()

    ARENA_BYTES = 164352
    arena_h = nc.alloc_sbuf_tensor("arena", [128, ARENA_BYTES], mybir.dt.uint8)
    AR = nc.lookup_mloc(arena_h).addr
    SBUF_TOP = AR + ARENA_BYTES

    def alloc(name, shape, dtype=F32, at=None):
        if at is None:
            return nc.alloc_sbuf_tensor(name, shape, dtype).ap()
        nbytes = int(np.prod(shape[1:])) * mybir.dt.size(dtype)
        assert at + nbytes <= SBUF_TOP, (name, at, nbytes)
        return nc.alloc_sbuf_tensor_at(name, shape, dtype, offset=at).ap()

    ident = alloc("identt", [128, 128])
    tri_lu = alloc("tri_lut", [128, 128])
    tri_li = alloc("tri_lit", [128, 128])
    normw_bc = alloc("normw_bct", [128, 128])
    normw_col = alloc("normw_colt", [128, 1])
    convw = alloc("convwt", [128, FQ * 4])
    dtbias_bc = alloc("dtbias_bct", [128, 32])
    alog_bc = alloc("alog_bct", [128, 32])
    negexpA = alloc("negexpA", [128, 32])
    scale_col = alloc("scale_colt", [128, 1])
    u8scr = alloc("u8scr", [128, 4], mybir.dt.uint8)
    scale1_col = alloc("scale1_colt", [128, 1])
    rscale_col = alloc("rscale_col", [128, 1])
    tmp_col = alloc("tmp_col", [128, 1])
    ones_col = alloc("ones_col", [128, 1])
    ones_row = alloc("ones_row", [1, 128])
    ones_full = alloc("ones_full", [128, 128])
    one_col = alloc("one_col", [128, 1])
    beta_tm = alloc("beta_tm", [128, NCH * 32])
    ld_tm = alloc("ld_tm", [128, NCH * 32])
    xn8 = alloc("xn8", [128, NDT * T], F8)
    sp_scr = alloc("sp_scr", [128, 64])
    wsp_h = alloc("wsp_h", [128, 512], BF16)
    wsp_l = alloc("wsp_l", [128, 512], BF16)

    xnT = alloc("xnT", [128, NDT * T], at=AR)
    xn_hl = alloc("xn_hl", [128, 2 * NDT * T], BF16, at=AR)
    SB = AR + NDT * T * 4
    sqbuf4 = alloc("sqbuf4", [128, TB], at=SB)
    murow = alloc("murow", [1, TB], at=SB + 2048)
    isrow = alloc("isrow", [1, TB], at=SB + 4096)
    musq_scr = alloc("musq_scr", [1, TB], at=SB + 6144)
    mu_bc = alloc("mu_bc", [128, TB], at=SB + 8192)
    is_bc = alloc("is_bc", [128, TB], at=SB + 10240)
    hi_scr = alloc("hi_scr", [128, TB], BF16, at=SB + 12288)
    lo_scr = alloc("lo_scr", [128, TB], BF16, at=SB + 13312)
    ph0_end = SB + 14336
    convstage = alloc("convstage", [128, T + 4], at=SB)
    wab_hl = alloc("wab_hl", [128, 2 * NDT * 64], BF16, at=SB)
    gatebuf = alloc("gatebuf", [128, T], at=SB + 8320)
    wblk = [alloc(f"wblk{i}", [128, HID], at=SB + 16512 + i * HID * 4) for i in range(2)]
    wblk_hl = [alloc(f"wblk_hl{i}", [128, 2 * HID], BF16, at=SB + 16512 + i * HID * 4)
               for i in range(2)]
    ph1_end = SB + 32896
    assert max(ph0_end, ph1_end) <= SBUF_TOP, (ph0_end, ph1_end)

    ch_sb = alloc("ch_sb", [128, FQ * 128], at=AR)

    b1 = AR + FQ * 128 * 4
    gate_ch = alloc("gate_ch", [128, FZ * 128], at=b1)
    b2 = b1 + FZ * 128 * 4
    outT_ch = alloc("outT_ch", [128, FZ * 128], at=b2)
    outT_ch_hl = alloc("outT_ch_hl", [128, 2 * FZ * 128], BF16, at=b2)
    b3 = b2 + FZ * 128 * 4
    S_sb = alloc("S_sb", [128, FZ * 128], at=b3)
    b4 = b3 + FZ * 128 * 4
    O_all = alloc("O_all", [128, FZ * 128], at=b4)
    b5 = b4 + FZ * 128 * 4
    smalls_p = [{}, {}]
    off = b5
    for par in range(2):
        for nm in ["L_tm", "Lm1_tm", "gm1_tm", "neggm1_tm", "negb_ch",
                   "colscalb_tm", "glast_bc", "ssq_tm", "rms_tm"]:
            smalls_p[par][nm] = alloc(f"{nm}_{par}", [128, 32], at=off); off += 512
    rn_row = alloc("rn_row", [1, 4096], at=off); off += 16384
    sqh = alloc("sqh", [128, 128], at=off); off += 512
    HT = [[{}, {}], [{}, {}]]
    for pp in range(2):
        for hi in range(2):
            for nm in ["Kn", "Qn", "Kt", "G", "QG"]:
                HT[pp][hi][nm] = alloc(f"ht_{nm}_{pp}{hi}", [128, 128], at=off); off += 512
    CT = [{} for _ in range(4)]
    for ci in range(4):
        for nm in ["Et_a", "Et_b", "Etm_a", "Etm_b", "Y_a", "Y_b", "ArgMt", "Expb",
                   "Tmp", "VtWsc"]:
            CT[ci][nm] = alloc(f"ct_{nm}_{ci}", [128, 128], at=off); off += 512
    sq_all = alloc("sq_all", [128, 2048], at=off); off += 8192
    ph2_end = off
    assert ph2_end <= SBUF_TOP, ph2_end

    outT_t = [alloc(f"outT_t{i}", [128, 2 * FZ * 128], BF16, at=AR + i * FZ * 128 * 4)
              for i in range(2)]
    c1 = AR + 2 * FZ * 128 * 4
    woutc_hl = alloc("woutc_hl", [128, 2 * 32 * 512], BF16, at=c1)
    c2 = c1 + 32 * 512 * 4
    q28 = alloc("q28", [128, NDT * HID], F8, at=c2)
    c2b = c2 + NDT * HID
    in2buf = [alloc(f"in2buf{i}", [128, HID], at=c2b + i * HID * 4) for i in range(2)]
    c3 = c2b + 2 * HID * 4
    xbuf = [alloc(f"xbuf{i}", [128, 512], at=c3 + i * 2048) for i in range(2)]
    c4 = c3 + 2 * 2048
    fbuf = [alloc(f"fbuf{i}", [128, 512], at=c4 + i * 2048) for i in range(2)]
    c5 = c4 + 2 * 2048
    ybuf = [alloc(f"ybuf{i}", [128, 512], at=c5 + i * 2048) for i in range(2)]
    c6 = c5 + 2 * 2048
    wout_in = [alloc(f"wout_in{i}", [128, 512], at=c6 + i * 2048) for i in range(2)]
    ph3_end = c6 + 2 * 2048
    assert ph3_end <= SBUF_TOP, ph3_end

    ps = [es.enter_context(nc.psum_tensor(f"ps{i}", [128, 512], F32)) for i in range(8)]

    sems = {}
    for e in ENGINES:
        sems[e] = es.enter_context(nc.semaphore(f"sem_{e}"))

    S = Sched()

    def dma_in(dst, src, slot, waits=()):
        return S.emit("sync", lambda e, d=dst, s=src: e.dma_start(out=d, in_=s),
                      waits=waits, slot=slot)

    def dma_out(dst, src, slot, waits=()):
        return S.emit("gpsimd", lambda e, d=dst, s=src: e.dma_start(out=d, in_=s),
                      waits=waits, slot=slot)

    def mm(pdst, lhsT, rhs, start, stop, waits=(), inc=0):
        return S.emit("tensor",
                      lambda e, o=pdst, l=lhsT, r=rhs, st=start, sp=stop:
                      e.matmul(o, l, r, start=st, stop=sp, skip_group_check=True),
                      waits=waits, inc=inc)

    def transp(pdst, src, waits=(), inc=0):
        return S.emit("tensor",
                      lambda e, o=pdst, s=src: e.transpose(o, s, ident),
                      waits=waits, inc=inc)

    def ve(fn, waits=(), inc=0):
        return S.emit("vector", fn, waits=waits, inc=inc)

    def se(fn, waits=(), inc=0):
        return S.emit("scalar", fn, waits=waits, inc=inc)

    t_consts = []
    t_consts.append(dma_in(ident[:, :], ident_in[:, :], "cst"))
    t_consts.append(dma_in(tri_lu[:, :], tri_lu_in[:, :], "cst"))
    t_consts.append(dma_in(tri_li[:, :], tri_li_in[:, :], "cst"))
    t_consts.append(dma_in(normw_bc[:, :], normw_bc_in[:, :], "cst"))
    t_consts.append(dma_in(normw_col[:, :], normw_col_in[:, :], "cst"))
    t_consts.append(dma_in(convw[:, :], convw_in[:, :], "cst"))
    t_consts.append(dma_in(dtbias_bc[:, :], dtbias_bc_in[:, :], "cst"))
    t_consts.append(dma_in(alog_bc[:, :], alog_bc_in[:, :], "cst"))
    t_consts.append(dma_in(scale_col[:, :], scale_col_in[:, :], "cst"))
    t_consts.append(dma_in(scale1_col[:, :], scale1_col_in[:, :], "cst"))
    t_cst0 = t_consts[-1]
    t_tc1 = ve(lambda e: e.tensor_copy(u8scr[:, :], scale_col[:, :].bitcast(mybir.dt.uint8)),
               waits=[t_cst0], inc=1)
    t_cst = ve(lambda e: e.tensor_copy(scale_col[:, :].bitcast(mybir.dt.uint8), u8scr[:, :]),
               waits=[t_tc1], inc=1)
    t_ones = ve(lambda e: e.memset(ones_col[:, :], 1.0), inc=1)
    t_ones2 = ve(lambda e: e.memset(ones_row[:, :], 1.0), inc=1)
    t_ones3 = ve(lambda e: e.memset(ones_full[:, :], 1.0), inc=1)
    t_one = ve(lambda e: e.memset(one_col[:, :], 1.0), inc=1)
    t_expA = se(lambda e: e.activation(negexpA[:, :], alog_bc[:, :], AF.Exp),
                waits=[t_cst], inc=1)
    t_negA = ve(lambda e: e.tensor_scalar_mul(negexpA[:, :], negexpA[:, :], -1.0),
                waits=[t_expA], inc=1)
    t_r1 = ve(lambda e: e.reciprocal(rscale_col[:, :], scale_col[:, :]),
              waits=[t_cst], inc=1)
    t_r2 = ve(lambda e: e.reciprocal(tmp_col[:, :], scale1_col[:, :]),
              waits=[t_cst], inc=1)
    t_rs = ve(lambda e: e.tensor_mul(rscale_col[:, :], rscale_col[:, :], tmp_col[:, :]),
              waits=[t_r1, t_r2], inc=1)

    t_xT_all = None
    for d in range(NDT):
        t_xT_all = dma_in(xnT[:, d * T:(d + 1) * T], xT[d * 128:(d + 1) * 128, :], "xT")

    t_mu_mm = None
    t_sq_mm = None
    for d in range(NDT):
        for tb in range(NTB):
            lastd = (d == NDT - 1)
            t_mu_mm = mm(ps[tb][:1, :], ones_col[:, :], xnT[:, d * T + tb * TB: d * T + (tb + 1) * TB],
                         start=(d == 0), stop=lastd,
                         waits=[t_xT_all, t_ones], inc=1 if (lastd and tb == NTB - 1) else 0)
        for tb in range(NTB):
            t_sqa = se(lambda e, d=d, tb=tb: e.activation(
                sqbuf4[:, :], xnT[:, d * T + tb * TB: d * T + (tb + 1) * TB], AF.Square),
                waits=[t_xT_all, t_sq_mm], inc=1)
            t_sq_mm = mm(ps[4 + tb][:1, :], ones_col[:, :], sqbuf4[:, :],
                         start=(d == 0), stop=(d == NDT - 1),
                         waits=[t_sqa], inc=1)
    t_xn = [None] * NDT
    t_xn8 = [None] * NDT
    t_prev_blk = None
    for tb in range(NTB):
        t_mur = ve(lambda e, tb=tb: e.tensor_scalar(murow[:1, :], ps[tb][:1, :],
                                                    1.0 / HID, None, ALU.mult, ALU.bypass),
                   waits=[t_mu_mm, t_prev_blk], inc=1)
        t1 = ve(lambda e, tb=tb: e.tensor_scalar(isrow[:1, :], ps[4 + tb][:1, :],
                                                 1.0 / HID, None, ALU.mult, ALU.bypass),
                waits=[t_sq_mm], inc=1)
        t2 = ve(lambda e: e.scalar_tensor_tensor(musq_scr[:1, :], murow[:1, :], 1.0,
                                                 murow[:1, :], ALU.mult, ALU.mult),
                waits=[t_mur, t1], inc=1)
        t3 = ve(lambda e: e.tensor_sub(isrow[:1, :], isrow[:1, :], musq_scr[:1, :]),
                waits=[t2], inc=1)
        t4 = ve(lambda e: e.tensor_scalar_add(isrow[:1, :], isrow[:1, :], 1e-5),
                waits=[t3], inc=1)
        t4b = ve(lambda e: e.reciprocal(musq_scr[:1, :], isrow[:1, :]), waits=[t4], inc=1)
        t_isr = se(lambda e: e.activation(isrow[:1, :], musq_scr[:1, :], AF.Sqrt),
                   waits=[t4b], inc=1)
        tmm1 = mm(ps[tb][:, :], ones_row[:, :], murow[:1, :], start=True, stop=True,
                  waits=[t_mur, t_ones2, t_mu_mm, t2], inc=1)
        t_mubc = se(lambda e, tb=tb: e.copy(mu_bc[:, :], ps[tb][:, :]), waits=[tmm1], inc=1)
        tmm2 = mm(ps[4 + tb][:, :], ones_row[:, :], isrow[:1, :], start=True, stop=True,
                  waits=[t_isr], inc=1)
        t_isbc = se(lambda e, tb=tb: e.copy(is_bc[:, :], ps[4 + tb][:, :]), waits=[tmm2], inc=1)
        if debug:
            dma_out(dbg[:, tb * TB:(tb + 1) * TB], mu_bc[:, :], "dbgd", waits=[t_mubc])
            dma_out(dbg[:, T + tb * TB:T + (tb + 1) * TB], is_bc[:, :], "dbgd", waits=[t_isbc])
        blk_done = None
        for d in range(NDT):
            sl = (d * T + tb * TB, d * T + (tb + 1) * TB)
            tsub = ve(lambda e, sl=sl: e.tensor_sub(xnT[:, sl[0]:sl[1]],
                                                    xnT[:, sl[0]:sl[1]], mu_bc[:, :]),
                      waits=[t_mubc, t_sq_mm if tb == 0 else None], inc=1)
            tmul = ve(lambda e, sl=sl: e.tensor_mul(xnT[:, sl[0]:sl[1]],
                                                    xnT[:, sl[0]:sl[1]], is_bc[:, :]),
                      waits=[tsub, t_isbc], inc=1)
            t8 = ve(lambda e, sl=sl, d=d, tb=tb: e.tensor_scalar(
                xn8[:, sl[0]:sl[1]], xnT[:, sl[0]:sl[1]], scale_col[:, :], None,
                ALU.mult, ALU.bypass),
                waits=[tmul, t_cst], inc=1)
            th_ = ve(lambda e, sl=sl: e.tensor_copy(hi_scr[:, :], xnT[:, sl[0]:sl[1]]),
                     waits=[t8], inc=1)
            tl_ = ve(lambda e, sl=sl: e.tensor_sub(lo_scr[:, :], xnT[:, sl[0]:sl[1]],
                                                   hi_scr[:, :]),
                     waits=[th_], inc=1)
            tch = ve(lambda e, d=d, tb=tb: e.tensor_copy(
                xhi(d, tb * TB, TB), hi_scr[:, :]), waits=[tl_], inc=1)
            tcl2 = ve(lambda e, d=d, tb=tb: e.tensor_copy(
                xlo(d, tb * TB, TB), lo_scr[:, :]), waits=[tch], inc=1)
            if tb == NTB - 1:
                t_xn[d] = tmul
                t_xn8[d] = t8
            blk_done = tcl2
        t_prev_blk = blk_done
    txn_all = t_prev_blk

    if debug:
        dma_out(xnT_dbg[:, :], xnT[:, :], "dbgd", waits=[txn_all])

    def xhi(d, t0, w):
        blk = t0 // TB
        o = 2 * (d * T + blk * TB) + (t0 - blk * TB)
        return xn_hl[:, o:o + w]

    def xlo(d, t0, w):
        blk = t0 // TB
        o = 2 * (d * T + blk * TB) + TB + (t0 - blk * TB)
        return xn_hl[:, o:o + w]

    def whl_hi(buf, d):
        o = 1024 * (d // 4) + (d % 4) * 128
        return wblk_hl[buf][:, o:o + 128]

    def whl_lo(buf, d):
        o = 1024 * (d // 4) + 512 + (d % 4) * 128
        return wblk_hl[buf][:, o:o + 128]

    t_ld = [None] * NCH
    ph0_bar = S.barrier_token()
    t_wab = dma_in(wblk[0][:, :NDT * 64], wab_l[:, :], "wb0", waits=ph0_bar)
    NAB = NDT * 64
    t_wabh = ve(lambda e: e.tensor_copy(wab_hl[:, 0:NAB], wblk[0][:, 0:NAB]),
                waits=[t_wab] + ph0_bar, inc=1)
    t_wabl = ve(lambda e: e.tensor_sub(wab_hl[:, NAB:2 * NAB], wblk[0][:, 0:NAB],
                                       wab_hl[:, 0:NAB]),
                waits=[t_wabh], inc=1)
    t_ab_ps = [None] * NCH
    t_beta = [None] * NCH
    t_ta = [None] * NCH
    for ti in range(NCH):
        for d in range(NDT):
            w0 = ([t_wabl, txn_all,
                   t_beta[ti - 2] if ti >= 2 else None,
                   t_ta[ti - 2] if ti >= 2 else None]
                  if d == 0 else [txn_all])
            for pi, (lh, rh) in enumerate([
                    (xhi(d, ti * 128, 128), wab_hl[:, d * 64:(d + 1) * 64]),
                    (xhi(d, ti * 128, 128), wab_hl[:, NAB + d * 64:NAB + (d + 1) * 64]),
                    (xlo(d, ti * 128, 128), wab_hl[:, d * 64:(d + 1) * 64])]):
                t_ab_ps[ti] = mm(ps[ti % 2][:, 0:64], lh, rh,
                                 start=(d == 0 and pi == 0),
                                 stop=(d == NDT - 1 and pi == 2),
                                 waits=w0 if pi == 0 else [],
                                 inc=1 if (d == NDT - 1 and pi == 2) else 0)
        t_beta[ti] = se(lambda e, ti=ti: e.activation(beta_tm[:, ti * 32:(ti + 1) * 32],
                                                      ps[ti % 2][:, 0:32], AF.Sigmoid),
                        waits=[t_ab_ps[ti]], inc=1)
        t_ta[ti] = ve(lambda e, ti=ti: e.tensor_add(ld_tm[:, ti * 32:(ti + 1) * 32],
                                                    ps[ti % 2][:, 32:64], dtbias_bc[:, :]),
                      waits=[t_ab_ps[ti], t_beta[ti]], inc=1)
    t_adump = None
    if debug:
        t_adump = dma_out(dbg[:, 2048:2080], ld_tm[:, 0:32], "dbgd", waits=[t_ta[0]])
    for ti in range(NCH):
        sl = (ti * 32, (ti + 1) * 32)
        ta = t_ta[ti]
        if ti == 0 and t_adump is not None:
            ta = t_adump
        tb_ = se(lambda e, ti=ti, sl=sl: e.activation(sp_scr[:, 0:32], ld_tm[:, sl[0]:sl[1]], AF.Abs),
                 waits=[ta, t_ld[ti - 1] if ti >= 1 else None], inc=1)
        tc = se(lambda e: e.activation(sp_scr[:, 32:64], sp_scr[:, 0:32], AF.Exp, scale=-1.0),
                waits=[tb_], inc=1)
        td = se(lambda e: e.activation(sp_scr[:, 0:32], sp_scr[:, 32:64], AF.Ln, bias=one_col[:, :]),
                waits=[tc, t_one], inc=1)
        te_ = ve(lambda e, ti=ti, sl=sl: e.tensor_scalar(sp_scr[:, 32:64], ld_tm[:, sl[0]:sl[1]],
                                                         0.0, None, ALU.max, ALU.bypass),
                 waits=[td], inc=1)
        tf = ve(lambda e: e.tensor_add(sp_scr[:, 0:32], sp_scr[:, 0:32], sp_scr[:, 32:64]),
                waits=[te_], inc=1)
        t_ld[ti] = ve(lambda e, ti=ti, sl=sl: e.tensor_mul(ld_tm[:, sl[0]:sl[1]],
                                                           sp_scr[:, 0:32], negexpA[:, :]),
                      waits=[tf, t_negA], inc=1)

    t_wdma = [None, None]
    t_wfree = [t_ab_ps[NCH - 1], None]
    t_whl = [None, None]
    wst = {"prev": None}
    t_convdma = []
    t_spill_last = None
    t_gate_last = None
    ph1_mm = None
    t_evac = {}

    def emit_wload(fb):
        buf = fb % 2
        src = wqkv_l[fb, :, :] if fb < FQ else wz_l[fb - FQ, :, :]
        t_wdma[buf] = dma_in(wblk[buf][:, :], src, f"wb{buf}",
                             waits=[t_wfree[buf], t_wabl if fb < 2 else None])
        last = None
        for j in range(4):
            th = ve(lambda e, buf=buf, j=j: e.tensor_copy(
                wsp_h[:, :], wblk[buf][:, j * 512:(j + 1) * 512]),
                waits=[t_wdma[buf], wst["prev"]] if j == 0 else [last], inc=1)
            tl = ve(lambda e, buf=buf, j=j: e.tensor_sub(
                wsp_l[:, :], wblk[buf][:, j * 512:(j + 1) * 512], wsp_h[:, :]),
                waits=[th], inc=1)
            tch2 = ve(lambda e, buf=buf, j=j: e.tensor_copy(
                wblk_hl[buf][:, 1024 * j:1024 * j + 512], wsp_h[:, :]),
                waits=[tl], inc=1)
            last = ve(lambda e, buf=buf, j=j: e.tensor_copy(
                wblk_hl[buf][:, 1024 * j + 512:1024 * j + 1024], wsp_l[:, :]),
                waits=[tch2], inc=1)
        t_whl[buf] = last
        wst["prev"] = last

    emit_wload(0)
    for fb in range(FQ + FZ):
        buf = fb % 2
        is_q = fb < FQ
        zi = fb - FQ
        if fb + 1 < FQ + FZ:
            emit_wload(fb + 1)
        tb_mm = [None] * NTB
        for tb in range(NTB):
            for d in range(NDT):
                w = [t_whl[buf]] if d == 0 else []
                if d == 0:
                    w.append(txn_all)
                    w.append(t_evac.get((fb - 1, tb)))
                    if fb == 0:
                        w.append(t_ld[NCH - 1])
                for pi, (lh, rh) in enumerate([
                        (whl_hi(buf, d), xhi(d, tb * TB, TB)),
                        (whl_hi(buf, d), xlo(d, tb * TB, TB)),
                        (whl_lo(buf, d), xhi(d, tb * TB, TB))]):
                    tb_mm[tb] = mm(ps[tb][:, :], lh, rh,
                                   start=(d == 0 and pi == 0),
                                   stop=(d == NDT - 1 and pi == 2),
                                   waits=w if pi == 0 else [],
                                   inc=1 if (d == NDT - 1 and pi == 2) else 0)
        if fb == FQ + FZ - 1:
            ph1_mm = tb_mm[NTB - 1]
        t_wfree[buf] = tb_mm[NTB - 1]
        if is_q:
            tz = ve(lambda e: e.memset(convstage[:, 0:4], 0.0),
                    waits=(ph0_bar if fb == 0 else []), inc=1)
            tev = [None] * NTB
            for tb in range(NTB):
                tev[tb] = se(lambda e, tb=tb: e.copy(convstage[:, 4 + tb * TB: 4 + (tb + 1) * TB],
                                                     ps[tb][:, :]),
                             waits=[tb_mm[tb], tz], inc=1)
                t_evac[(fb, tb)] = tev[tb]
            cws = [convw[:, fb * 4 + j: fb * 4 + j + 1] for j in range(4)]
            tc0 = ve(lambda e, cw=cws[0]: e.tensor_scalar(gatebuf[:, :], convstage[:, 1:1 + T],
                                                          cw, None, ALU.mult, ALU.bypass),
                     waits=[tev[0], tev[1], tev[2], tev[3], t_spill_last], inc=1)
            tcj = tc0
            for j in range(1, 4):
                tcj = ve(lambda e, j=j, cw=cws[j]: e.scalar_tensor_tensor(
                    gatebuf[:, :], convstage[:, 1 + j:1 + j + T], cw, gatebuf[:, :],
                    ALU.mult, ALU.add),
                    waits=[tcj], inc=1)
            srcv = gatebuf[:, :].rearrange("p (c t) -> p c t", c=NCH)
            for cg in range(4):
                t_spill_last = dma_out(
                    qkvc_sp[cg * 4:(cg + 1) * 4, :, fb * 128:(fb + 1) * 128].rearrange("c p t -> p c t"),
                    srcv[:, cg * 4:(cg + 1) * 4, :], "qsp",
                    waits=[tcj])
        else:
            tev = [None] * NTB
            for tb in range(NTB):
                tsil = se(lambda e, tb=tb: e.activation(gatebuf[:, tb * TB:(tb + 1) * TB],
                                                        ps[tb][:, :], AF.Silu),
                          waits=[tb_mm[tb], t_gate_last, t_spill_last if zi == 0 else None],
                          inc=1)
                tev[tb] = ve(lambda e, tb=tb: e.tensor_scalar(gatebuf[:, tb * TB:(tb + 1) * TB],
                                                              gatebuf[:, tb * TB:(tb + 1) * TB],
                                                              normw_col[:, :], None,
                                                              ALU.mult, ALU.bypass),
                             waits=[tsil], inc=1)
                t_evac[(fb, tb)] = tsil
            srcv = gatebuf[:, :].rearrange("p (c t) -> p c t", c=NCH)
            for cg in range(4):
                t_gate_last = dma_out(
                    gate_sp[cg * 4:(cg + 1) * 4, :, zi * 128:(zi + 1) * 128].rearrange("c p t -> p c t"),
                    srcv[:, cg * 4:(cg + 1) * 4, :], "gsp",
                    waits=tev)

    ph1_bar = S.barrier_token()

    t_Sinit = ve(lambda e: e.memset(S_sb[:, :], 0.0), waits=ph1_bar, inc=1)
    t_out_dma = None
    t_chain_last_prev = {}
    t_prep_free_prev = {}
    NCH_RUN = 1 if only00 else NCH
    NK_RUN = 1 if only00 else NK
    DEFER_BND = True
    late_pending = [None]
    t_S_prev = {}
    pst = {}
    t_ep_prev = {}

    def make_boundary_thunks(c2, vt_tok):
        sm2 = smalls_p[c2 % 2]
        ld2 = ld_tm[:, c2 * 32:(c2 + 1) * 32]
        beta2 = beta_tm[:, c2 * 32:(c2 + 1) * 32]
        first = ph1_bar if c2 == 0 else []
        bn = {}
        th = []

        def b_dma():
            bn["chd"] = dma_in(ch_sb[:, :], qkvc_sp[c2, :, :], "ch",
                               waits=list(first) + vt_tok())
        th.append(b_dma)

        def b_tL():
            bn["tL"] = mm(ps[0][:, 0:32], tri_li[:, :], ld2, start=True, stop=True,
                          waits=[t_ld[c2], t_cst if c2 == 0 else None,
                                 pst.get("G"), pst.get("Kt")] + list(first), inc=1)
        th.append(b_tL)
        th.append(lambda: bn.__setitem__("tLe", ve(
            lambda e: e.tensor_copy(sm2["L_tm"][:, :], ps[0][:, 0:32]),
            waits=[bn["tL"]], inc=1)))
        th.append(lambda: bn.__setitem__("tLm", ve(
            lambda e: e.tensor_sub(sm2["Lm1_tm"][:, :], sm2["L_tm"][:, :], ld2),
            waits=[bn["tLe"]], inc=1)))
        th.append(lambda: bn.__setitem__("tgm", se(
            lambda e: e.activation(sm2["gm1_tm"][:, :], sm2["Lm1_tm"][:, :], AF.Exp),
            waits=[bn["tLm"]], inc=1)))
        th.append(lambda: bn.__setitem__("tngm", ve(
            lambda e: e.tensor_scalar_mul(sm2["neggm1_tm"][:, :], sm2["gm1_tm"][:, :], -1.0),
            waits=[bn["tgm"]], inc=1)))
        th.append(lambda: bn.__setitem__("tnb", ve(
            lambda e: e.tensor_scalar_mul(sm2["negb_ch"][:, :], beta2, -1.0),
            waits=[t_beta[c2]], inc=1)))
        th.append(lambda: bn.__setitem__("tLl", mm(
            ps[0][:, 32:64], ones_full[:, :], ld2, start=True, stop=True,
            waits=[bn["tLe"], t_ones3], inc=1)))
        th.append(lambda: bn.__setitem__("tcs1", ve(
            lambda e: e.tensor_sub(sm2["colscalb_tm"][:, :], ps[0][:, 32:64], sm2["L_tm"][:, :]),
            waits=[bn["tLl"], bn["tLe"]], inc=1)))
        th.append(lambda: bn.__setitem__("tcs2", se(
            lambda e: e.activation(sm2["colscalb_tm"][:, :], sm2["colscalb_tm"][:, :], AF.Exp),
            waits=[bn["tcs1"]], inc=1)))
        th.append(lambda: bn.__setitem__("tcs3", ve(
            lambda e: e.tensor_mul(sm2["colscalb_tm"][:, :], sm2["colscalb_tm"][:, :], beta2),
            waits=[bn["tcs2"]], inc=1)))
        th.append(lambda: bn.__setitem__("tgl1", ve(
            lambda e: e.tensor_copy(sm2["glast_bc"][:, :], ps[0][:, 32:64]),
            waits=[bn["tLl"], bn["tcs1"]], inc=1)))
        th.append(lambda: bn.__setitem__("tgl2", se(
            lambda e: e.activation(sm2["glast_bc"][:, :], sm2["glast_bc"][:, :], AF.Exp),
            waits=[bn["tgl1"]], inc=1)))
        for h in range(NK_RUN):
            kof = (NK + h) * 128
            qof = h * 128

            def b_sq1(h=h, kof=kof):
                bn["sq1"] = se(lambda e: e.activation(sqh[:, :], ch_sb[:, kof:kof + 128],
                                                      AF.Square),
                               waits=[bn["chd"], bn.get("m2")], inc=1)
            th.append(b_sq1)

            def b_m1(h=h):
                bn["m1"] = mm(ps[1][:1, 256:384], ones_col[:, :], sqh[:, :],
                              start=True, stop=True,
                              waits=[bn["sq1"], bn.get("ev"),
                                     pst.get("QG") if h == 0 else None]
                                    + (first if h == 0 else []),
                              inc=1)
            th.append(b_m1)

            def b_sq2(h=h, qof=qof):
                bn["sq2"] = se(lambda e: e.activation(sqh[:, :], ch_sb[:, qof:qof + 128],
                                                      AF.Square),
                               waits=[bn["m1"]], inc=1)
            th.append(b_sq2)

            def b_m2(h=h):
                bn["m2"] = mm(ps[1][:1, 384:512], ones_col[:, :], sqh[:, :],
                              start=True, stop=True, waits=[bn["sq2"]], inc=1)
            th.append(b_m2)

            def b_ev(h=h):
                bn["ev"] = ve(lambda e: e.tensor_copy(
                    rn_row[:1, h * 256:(h + 1) * 256], ps[1][:1, 256:512]),
                    waits=[bn["m2"],
                           pst.get("Kn") if h == 0 else None,
                           pst.get("Qn") if h == 0 else None], inc=1)
            th.append(b_ev)

        def b_fin():
            tcl = ve(lambda e: e.tensor_scalar(rn_row[:1, :], rn_row[:1, :], 1e-24, None,
                                               ALU.max, ALU.bypass),
                     waits=[bn["ev"]], inc=1)
            tln = se(lambda e: e.activation(rn_row[:1, :], rn_row[:1, :], AF.Ln),
                     waits=[tcl], inc=1)
            bn["trs"] = se(lambda e: e.activation(rn_row[:1, :], rn_row[:1, :], AF.Exp,
                                                  scale=-0.5),
                           waits=[tln], inc=1)
            pst["G"] = bn["tcs1"]
            pst["Kt"] = bn["tgl1"]
            pst["QG"] = bn["ev"]
            pst.pop("Kn", None)
            pst.pop("Qn", None)
        th.append(b_fin)
        n_early = 13
        return bn, th[1:n_early] , [th[0]] + th[n_early:]

    bnd_next = None
    for c in range(NCH_RUN):
        Sin_t = S_sb
        Sout_t = S_sb
        csl = (c * 32, (c + 1) * 32)
        ld_ch = ld_tm[:, csl[0]:csl[1]]
        beta_ch = beta_tm[:, csl[0]:csl[1]]
        sm = smalls_p[c % 2]
        if c == 0 or not DEFER_BND:
            bnd, bth_e, bth_l = make_boundary_thunks(
                c, (lambda: []) if c == 0
                else (lambda vw=[t_S_prev[(NK_RUN - 1, 0)], t_S_prev[(NK_RUN - 1, 1)]]: vw))
            for f_ in bth_e + bth_l:
                f_()
        else:
            bnd = bnd_next
        t_chd = bnd["chd"]
        tLe = bnd["tLe"]
        tngm = bnd["tngm"]
        tnb = bnd["tnb"]
        tcs3 = bnd["tcs3"]
        tgl2 = bnd["tgl2"]
        trs = bnd["trs"]
        t_gd = dma_in(gate_ch[:, :], gate_sp[c, :, :], "gch",
                      waits=(list(ph1_bar) if c == 0
                             else [t_ep_prev[hp] for hp in range(2 * NK_RUN)]))
        if only00:
            tnb = ve(lambda e, sm=sm: e.memset(sm["ssq_tm"][:, :], 1.0), waits=[tnb], inc=1)
            tnb = ve(lambda e: e.memset(outT_ch[:, :], 0.0), waits=[tnb], inc=1)

        t_S_upd = {}
        t_O_ev = {}
        t_chain_last = dict(t_chain_last_prev)
        t_prep_free = dict(t_prep_free_prev)
        npairs = (NK_RUN + 1) // 2

        def make_prep_thunks(pair):
            hs_ = [h_ for h_ in (2 * pair, 2 * pair + 1) if h_ < NK_RUN]
            pp_ = pair % 2
            ptoks = [dict() for _ in hs_]
            thunks = []
            for hi, h in enumerate(hs_):
                T_ = HT[pp_][hi]
                kof = (NK + h) * 128
                qof = h * 128
                free_w = [t_prep_free.get((pp_, hi))]
                pt = ptoks[hi]
                prevp = dict(pst)

                def th_mk(T_=T_, h=h, pt=pt, free_w=free_w, prevp=prevp):
                    pt["_mk"] = mm(ps[0][:, 0:128], ones_row[:, :],
                                   rn_row[:1, h * 256: h * 256 + 128],
                                   start=True, stop=True,
                                   waits=[trs, pst.get("G"), pst.get("Kt")] + free_w,
                                   inc=1)

                def th_Kn(T_=T_, kof=kof, pt=pt, free_w=free_w):
                    pt["Kn"] = ve(lambda e: e.tensor_mul(
                        T_["Kn"][:, :], ch_sb[:, kof:kof + 128], ps[0][:, 0:128]),
                        waits=[pt["_mk"]] + free_w, inc=1)

                def th_mq(T_=T_, h=h, pt=pt):
                    pt["_mq"] = mm(ps[1][:, 0:128], ones_row[:, :],
                                   rn_row[:1, h * 256 + 128: h * 256 + 256],
                                   start=True, stop=True,
                                   waits=[trs, pt["Kn"], pst.get("QG")], inc=1)

                def th_Qn(T_=T_, qof=qof, pt=pt):
                    pt["Qn"] = ve(lambda e: e.tensor_mul(
                        T_["Qn"][:, :], ch_sb[:, qof:qof + 128], ps[1][:, 0:128]),
                        waits=[pt["_mq"]], inc=1)

                def th_mg(T_=T_, pt=pt):
                    pt["_mg"] = mm(ps[0][:, 0:128], T_["Kn"][:, :], T_["Kn"][:, :],
                                   start=True, stop=True, waits=[pt["Kn"]], inc=1)

                def th_Ge(T_=T_, pt=pt, free_w=free_w):
                    pt["G"] = se(lambda e: e.copy(T_["G"][:, :], ps[0][:, 0:128]),
                                 waits=[pt["_mg"]] + free_w, inc=1)

                def th_mqg(T_=T_, pt=pt):
                    pt["_mqg"] = mm(ps[1][:, 0:128], T_["Kn"][:, :], T_["Qn"][:, :],
                                    start=True, stop=True, waits=[pt["Qn"], pt["G"]], inc=1)

                def th_QGe(T_=T_, pt=pt):
                    pt["QG"] = se(lambda e: e.copy(T_["QG"][:, :], ps[1][:, 0:128]),
                                  waits=[pt["_mqg"]], inc=1)

                def th_mkt(T_=T_, pt=pt):
                    pt["_mkt"] = transp(ps[0][:, 128:256], T_["Kn"][:, :],
                                        waits=[pt["G"], t_cst], inc=1)

                def th_Kte(T_=T_, pt=pt):
                    pt["Kt"] = se(lambda e: e.copy(T_["Kt"][:, :], ps[0][:, 128:256]),
                                  waits=[pt["_mkt"]], inc=1)
                    pst.clear()
                    pst.update(pt)

                thunks += [th_mk, th_Kn, th_mq, th_Qn, th_mg, th_Ge, th_mqg, th_QGe,
                           th_mkt, th_Kte]
            return ptoks, thunks

        prep_pending = None
        for pair in range(npairs):
            hs = [h_ for h_ in (2 * pair, 2 * pair + 1) if h_ < NK_RUN]
            pp = pair % 2
            if pair == 0:
                ptoks0, thunks0 = make_prep_thunks(0)
                for th in thunks0:
                    th()
                prep_toks = ptoks0
            else:
                prep_toks = prep_pending
            bmode = False
            if pair + 1 < npairs:
                prep_pending, next_thunks = make_prep_thunks(pair + 1)
            else:
                next_thunks = []
                if DEFER_BND and c + 1 < NCH_RUN:
                    bnd_next, next_thunks, late_thunks = make_boundary_thunks(
                        c + 1,
                        lambda: [t_S_upd[(h_, p2)] for h_ in hs for p2 in range(2)])
                    bmode = True
            nthi = 0
            prep_tok = {hi: prep_toks[hi] for hi in range(len(hs))}
            chains = [(hi, h, p_) for hi, h in enumerate(hs) for p_ in range(2)]
            tok = {}
            NLEV = 7
            NST = 12 + 6 * NLEV + 14
            for st in range(NST):
                if bmode:
                    nemit = 0 if st < 9 else (3 if st == 9 else (1 if st < 40 else 4))
                    while nemit > 0 and nthi < len(next_thunks):
                        next_thunks[nthi]()
                        nthi += 1
                        nemit -= 1
                elif st % 3 == 0 and nthi < len(next_thunks):
                    next_thunks[nthi]()
                    nthi += 1
                for ci, (hi, h, p_) in enumerate(chains):
                    hp = 2 * h + p_
                    d = CT[ci]
                    T_ = HT[pp][hi]
                    pk = prep_tok[hi]
                    B_ = ps[2 + ci]
                    R0 = B_[:, 0:128]
                    R1 = B_[:, 128:256]
                    Ssl = Sin_t[:, hp * 128:(hp + 1) * 128]
                    Sosl = Sout_t[:, hp * 128:(hp + 1) * 128]
                    Lcol = sm["L_tm"][:, hp:hp + 1]
                    vof = (2 * NK + hp) * 128
                    k = ci
                    reuse = [t_chain_last.get(ci)]
                    if st == 0:
                        tok[(k, "trild")] = ve(lambda e, d=d, hp=hp, ld_ch=ld_ch: e.tensor_scalar(
                            d["Tmp"][:, :], tri_lu[:, :], ld_ch[:, hp:hp + 1], None,
                            ALU.mult, ALU.bypass),
                            waits=[t_ld[c], t_cst, tLe] + reuse, inc=1)
                    elif st == 1:
                        tok[(k, "lm1bc")] = mm(R0, ones_full[:, :], d["Tmp"][:, :],
                                               start=True, stop=True,
                                               waits=[tok[(k, "trild")]] + reuse, inc=1)
                    elif st == 2:
                        tok[(k, "argD")] = ve(lambda e, d=d, R0=R0, Lcol=Lcol: e.tensor_scalar(
                            d["ArgMt"][:, :], R0, Lcol, 0.0, ALU.subtract, ALU.min),
                            waits=[tok[(k, "lm1bc")]] + reuse, inc=1)
                    elif st == 3:
                        tok[(k, "expD")] = se(lambda e, d=d: e.activation(
                            d["Expb"][:, :], d["ArgMt"][:, :], AF.Exp),
                            waits=[tok[(k, "argD")], trs] + reuse, inc=1)
                    elif st == 4:
                        tok[(k, "et1")] = ve(lambda e, d=d, T_=T_: e.tensor_mul(
                            d["Tmp"][:, :], d["Expb"][:, :], T_["G"][:, :]),
                            waits=[tok[(k, "expD")], pk["G"]], inc=1)
                    elif st == 5:
                        tok[(k, "Et0")] = ve(lambda e, d=d, hp=hp, sm=sm: e.scalar_tensor_tensor(
                            d["Et_a"][:, :], d["Tmp"][:, :], sm["negb_ch"][:, hp:hp + 1],
                            tri_lu[:, :], ALU.mult, ALU.mult),
                            waits=[tok[(k, "et1")], tnb], inc=1)
                    elif st == 6:
                        tok[(k, "EtT")] = transp(R0, d["Et_a"][:, :],
                                                 waits=[tok[(k, "Et0")], tok[(k, "argD")]], inc=1)
                    elif st == 7:
                        tok[(k, "Etm0")] = se(lambda e, d=d, R0=R0: e.copy(d["Etm_a"][:, :], R0),
                                              waits=[tok[(k, "EtT")]], inc=1)
                    elif st == 8:
                        tok[(k, "VtT")] = transp(R1, ch_sb[:, vof:vof + 128],
                                                 waits=[t_chd, tok[(k, "Etm0")]] + reuse, inc=1)
                    elif st == 9:
                        tok[(k, "Vte")] = se(lambda e, d=d, R1=R1: e.copy(d["VtWsc"][:, :], R1),
                                             waits=[tok[(k, "VtT")]], inc=1)
                    elif st == 10:
                        tok[(k, "KS")] = mm(R0, T_["Kn"][:, :], Ssl, start=True, stop=True,
                                            waits=[pk["Kn"], tok[(k, "Etm0")],
                                                   t_Sinit if c == 0 else None,
                                                   t_S_prev.get((h, p_))], inc=1)
                    elif st == 11:
                        tok[(k, "B")] = ve(lambda e, d=d, R0=R0, hp=hp, sm=sm: e.scalar_tensor_tensor(
                            d["Y_a"][:, :], R0, sm["neggm1_tm"][:, hp:hp + 1],
                            d["VtWsc"][:, :], ALU.mult, ALU.add),
                            waits=[tok[(k, "KS")], tok[(k, "Vte")], tngm], inc=1)
                    elif 12 <= st < 12 + 6 * NLEV:
                        lev, sub = divmod(st - 12, 6)
                        Ya = d["Y_a"] if lev % 2 == 0 else d["Y_b"]
                        Yb = d["Y_b"] if lev % 2 == 0 else d["Y_a"]
                        Eta = d["Et_a"] if lev % 2 == 0 else d["Et_b"]
                        Etb = d["Et_b"] if lev % 2 == 0 else d["Et_a"]
                        Etma = d["Etm_a"] if lev % 2 == 0 else d["Etm_b"]
                        Etmb = d["Etm_b"] if lev % 2 == 0 else d["Etm_a"]
                        if sub == 0:
                            prevY = tok[(k, "B")] if lev == 0 else tok[(k, f"add{lev-1}")]
                            prevE = tok[(k, "Etm0")] if lev == 0 else tok[(k, f"tre{lev-1}")]
                            tok[(k, f"ap{lev}")] = mm(R0, Eta[:, :], Ya[:, :],
                                                      start=True, stop=True,
                                                      waits=[prevY, prevE], inc=1)
                        elif sub == 1:
                            tok[(k, f"add{lev}")] = ve(lambda e, Ya=Ya, Yb=Yb, R0=R0:
                                                       e.tensor_add(Yb[:, :], R0, Ya[:, :]),
                                                       waits=[tok[(k, f"ap{lev}")]], inc=1)
                        elif lev < 6:
                            if sub == 2:
                                tok[(k, f"sq{lev}")] = mm(R1, Etma[:, :], Eta[:, :],
                                                          start=True, stop=True,
                                                          waits=[tok[(k, f"add{lev}")],
                                                                 tok.get((k, f"tre{lev-1}")),
                                                                 tok[(k, "Vte")] if lev == 0 else None],
                                                          inc=1)
                            elif sub == 3:
                                tok[(k, f"sqe{lev}")] = se(lambda e, Etb=Etb, R1=R1:
                                                           e.copy(Etb[:, :], R1),
                                                           waits=[tok[(k, f"sq{lev}")]], inc=1)
                            elif sub == 4:
                                tok[(k, f"tr{lev}")] = transp(R0, Etb[:, :],
                                                              waits=[tok[(k, f"sqe{lev}")],
                                                                     tok[(k, f"add{lev}")]], inc=1)
                            elif sub == 5:
                                tok[(k, f"tre{lev}")] = se(lambda e, Etmb=Etmb, R0=R0:
                                                           e.copy(Etmb[:, :], R0),
                                                           waits=[tok[(k, f"tr{lev}")]], inc=1)
                    else:
                        fst = st - 12 - 6 * NLEV
                        Wbuf = d["Y_b"] if (NLEV - 1) % 2 == 0 else d["Y_a"]
                        tokW = tok[(k, f"add{NLEV-1}")]
                        if fst == 0:
                            tok[(k, "trild2")] = ve(lambda e, d=d, hp=hp, ld_ch=ld_ch: e.tensor_scalar(
                                d["Tmp"][:, :], tri_li[:, :], ld_ch[:, hp:hp + 1], None,
                                ALU.mult, ALU.bypass),
                                waits=[tokW], inc=1)
                        elif fst == 1:
                            tok[(k, "lbc")] = mm(R0, ones_full[:, :], d["Tmp"][:, :],
                                                 start=True, stop=True,
                                                 waits=[tok[(k, "trild2")],
                                                        tok[(k, f"tre{NLEV-2}")]], inc=1)
                        elif fst == 2:
                            tok[(k, "argM")] = ve(lambda e, d=d, R0=R0, Lcol=Lcol: e.tensor_scalar(
                                d["ArgMt"][:, :], R0, Lcol, 0.0, ALU.subtract, ALU.min),
                                waits=[tok[(k, "lbc")], tok[(k, "expD")]], inc=1)
                        elif fst == 3:
                            tok[(k, "expM")] = se(lambda e, d=d: e.activation(
                                d["Expb"][:, :], d["ArgMt"][:, :], AF.Exp),
                                waits=[tok[(k, "argM")]], inc=1)
                        elif fst == 4:
                            tok[(k, "mt1")] = ve(lambda e, d=d, T_=T_: e.tensor_mul(
                                d["Tmp"][:, :], d["Expb"][:, :], T_["QG"][:, :]),
                                waits=[tok[(k, "expM")], pk["QG"], tok[(k, "lbc")]], inc=1)
                        elif fst == 5:
                            tok[(k, "Mt")] = ve(lambda e, d=d, hp=hp, beta_ch=beta_ch:
                                                e.scalar_tensor_tensor(
                                d["ArgMt"][:, :], d["Tmp"][:, :], beta_ch[:, hp:hp + 1],
                                tri_li[:, :], ALU.mult, ALU.mult),
                                waits=[tok[(k, "mt1")], tok[(k, "expM")]], inc=1)
                        elif fst == 6:
                            tok[(k, "gbc")] = se(lambda e, d=d, R0=R0: e.activation(
                                d["Expb"][:, :], R0, AF.Exp),
                                waits=[tok[(k, "lbc")], tok[(k, "mt1")]], inc=1)
                        elif fst == 7:
                            tok[(k, "Qg")] = ve(lambda e, d=d, T_=T_: e.tensor_mul(
                                d["Tmp"][:, :], T_["Qn"][:, :], d["Expb"][:, :]),
                                waits=[tok[(k, "gbc")], pk["Qn"], tok[(k, "Mt")]], inc=1)
                        elif fst == 8:
                            o1 = mm(R1, d["Tmp"][:, :], Ssl, start=True, stop=False,
                                    waits=[tok[(k, "Qg")], tok[(k, "KS")],
                                           tok[(k, f"sqe{NLEV-2}")],
                                           t_S_prev.get((h, p_))], inc=0)
                            tok[(k, "Omm")] = mm(R1, d["ArgMt"][:, :], Wbuf[:, :],
                                                 start=False, stop=True,
                                                 waits=[], inc=1)
                        elif fst == 9:
                            t_O_ev[(h, p_)] = se(lambda e, R1=R1, hp=hp: e.copy(
                                O_all[:, hp * 128:(hp + 1) * 128], R1),
                                waits=[tok[(k, "Omm")]], inc=1)
                        elif fst == 10:
                            tsq_ = ve(lambda e, d=d, hp=hp: e.tensor_mul(
                                d["Expb"][:, :], O_all[:, hp * 128:(hp + 1) * 128],
                                O_all[:, hp * 128:(hp + 1) * 128]),
                                waits=[t_O_ev[(h, p_)]], inc=1)
                            tok[(k, "ssq")] = ve(lambda e, d=d, hp=hp, sm=sm: e.reduce_sum(
                                sm["ssq_tm"][:, hp:hp + 1], d["Expb"][:, :], AX.X),
                                waits=[tsq_], inc=1)
                        elif fst == 11:
                            tok[(k, "Wsc")] = ve(lambda e, d=d, hp=hp, Wbuf=Wbuf, sm=sm: e.tensor_scalar(
                                d["VtWsc"][:, :], Wbuf[:, :], sm["colscalb_tm"][:, hp:hp + 1],
                                None, ALU.mult, ALU.bypass),
                                waits=[tokW, tcs3, tok[(k, "B")]], inc=1)
                        elif fst == 12:
                            tok[(k, "msu")] = mm(R0, T_["Kt"][:, :], d["VtWsc"][:, :],
                                                 start=True, stop=True,
                                                 waits=[pk["Kt"], tok[(k, "Wsc")],
                                                        t_O_ev[(h, p_)], tok[(k, "gbc")]], inc=1)
                        elif fst == 13:
                            t_S_upd[(h, p_)] = ve(
                                lambda e, R0=R0, Ssl=Ssl, Sosl=Sosl, hp=hp, sm=sm:
                                e.scalar_tensor_tensor(Sosl, Ssl, sm["glast_bc"][:, hp:hp + 1],
                                                       R0, ALU.mult, ALU.add),
                                waits=[tok[(k, "msu")], tgl2], inc=1)
                            t_chain_last[ci] = t_S_upd[(h, p_)]
            while nthi < len(next_thunks):
                next_thunks[nthi]()
                nthi += 1
            if bmode:
                for f_ in late_thunks:
                    f_()
            for hi, h in enumerate(hs):
                t_prep_free[(pp, hi)] = t_S_upd[(h, 1)]
        t_chain_last_prev = t_chain_last
        t_prep_free_prev = t_prep_free
        trmA = ve(lambda e, sm=sm: e.tensor_scalar(sm["rms_tm"][:, :], sm["ssq_tm"][:, :],
                                            1.0 / DV, 1e-6, ALU.mult, ALU.add),
                  waits=[t_S_upd[(NK_RUN - 1, 0)], t_S_upd[(NK_RUN - 1, 1)],
                         t_O_ev[(NK_RUN - 1, 0)], t_O_ev[(NK_RUN - 1, 1)]],
                  inc=1)
        trm0 = ve(lambda e, sm=sm: e.reciprocal(sm["ssq_tm"][:, :], sm["rms_tm"][:, :]),
                  waits=[trmA], inc=1)
        trm = se(lambda e, sm=sm: e.activation(sm["rms_tm"][:, :], sm["ssq_tm"][:, :], AF.Sqrt),
                 waits=[trm0], inc=1)
        t_ep = {}
        for hp in range(2 * NK_RUN):
            bE = ps[6 + hp % 2]
            osl = O_all[:, hp * 128:(hp + 1) * 128]
            t1 = ve(lambda e, osl=osl, hp=hp, sm=sm: e.tensor_scalar(
                CT[hp % 4]["Tmp"][:, :], osl, sm["rms_tm"][:, hp:hp + 1], None,
                ALU.mult, ALU.bypass),
                waits=[trm, t_O_ev[(hp // 2, hp % 2)], t_ep.get(hp - 4),
                       t_chain_last.get(hp % 4)], inc=1)
            t2 = transp(bE[:, 0:128], CT[hp % 4]["Tmp"][:, :],
                        waits=[t1, t_ep.get(hp - 2)], inc=1)
            ep1 = ve(lambda e, bE=bE, hp=hp: e.tensor_mul(
                CT[hp % 4]["Et_a"][:, 0:128], bE[:, 0:128],
                gate_ch[:, hp * 128:(hp + 1) * 128]),
                waits=[t2, t_gd], inc=1)
            eph = ve(lambda e, hp=hp: e.tensor_copy(
                outT_ch_hl[:, hp * 256:hp * 256 + 128], CT[hp % 4]["Et_a"][:, 0:128]),
                waits=[ep1, t_out_dma], inc=1)
            t_ep[hp] = ve(lambda e, hp=hp: e.tensor_sub(
                outT_ch_hl[:, hp * 256 + 128:hp * 256 + 256],
                CT[hp % 4]["Et_a"][:, 0:128],
                outT_ch_hl[:, hp * 256:hp * 256 + 128]),
                waits=[eph], inc=1)
        if debug and c == 0:
            dma_out(dbg[:, 2176:2304], Sout_t[:, 0:128], "dbgd", waits=[t_S_upd[(0, 0)]])
            dma_out(dbg[:, 2304:2432], Sout_t[:, 128:256], "dbgd", waits=[t_S_upd[(0, 1)]])
        t_out_dma = dma_out(outT_sp[c, :, :], outT_ch_hl[:, :], "osp",
                            waits=[t_ep[hp] for hp in range(2 * NK_RUN)])
        t_ep_prev = t_ep
        t_S_prev = dict(t_S_upd)

    ph2_bar = S.barrier_token()

    if not only00:
        t_q2 = [None] * NDT
        t_i2d = [None, None]
        for ki in range(NDT):
            buf = ki % 2
            t_i2d[buf] = dma_in(in2buf[buf][:, :], in2[ki * 128:(ki + 1) * 128, :],
                                f"i2{buf}",
                                waits=[t_q2[ki - 2]] + (ph2_bar if ki < 2 else []))
            t_q2[ki] = ve(lambda e, ki=ki, buf=buf: e.tensor_scalar(
                q28[:, ki * HID:(ki + 1) * HID], in2buf[buf][:, :], scale1_col[:, :], None,
                ALU.mult, ALU.bypass),
                waits=[t_i2d[buf]], inc=1)

        t_y_st = []
        t_wc = None
        t_ot = [None, None]
        t_ot_free = [None, None]
        t_xd = [None, None]
        t_ybufree = {}
        for cb in range(4):
            t_wcs = {}
            for ki in range(32):
                jb = ki % 2
                t_wcd = dma_in(wout_in[jb][:, :], wout_l[cb, ki, :, :], f"wc{jb}",
                               waits=[t_wcs.get(ki - 2)]
                                     + (ph2_bar if cb == 0 and ki < 2 else []))
                twh2 = ve(lambda e, ki=ki, jb=jb: e.tensor_copy(
                    woutc_hl[:, ki * 512:(ki + 1) * 512], wout_in[jb][:, :]),
                    waits=[t_wcd, t_ot_free[0], t_ot_free[1]], inc=1)
                t_wcs[ki] = ve(lambda e, ki=ki, jb=jb: e.tensor_sub(
                    woutc_hl[:, 16384 + ki * 512:16384 + (ki + 1) * 512],
                    wout_in[jb][:, :], woutc_hl[:, ki * 512:(ki + 1) * 512]),
                    waits=[twh2], inc=1)
            t_wc_done = t_wcs[31]
            for ti in range(NCH):
                buf = ti % 2
                t_ot[buf] = dma_in(outT_t[buf][:, :], outT_sp[ti, :, :], f"ot{buf}",
                                   waits=[t_ot_free[buf]] + (ph2_bar if cb == 0 and ti < 2 else []))
                t_xd[buf] = dma_in(xbuf[buf][:, :], x_nat[ti * 128:(ti + 1) * 128, cb * 512:(cb + 1) * 512],
                                   f"xb{buf}",
                                   waits=[t_ybufree.get(16 * cb + ti - 2)])
                pw = ps[buf]
                pf = ps[2 + buf]
                mmw = None
                w0 = ([t_ot[buf], t_wc_done, t_ybufree.get(16 * cb + ti - 2)]
                      + (ph2_bar if cb == 0 and ti == 0 else []))
                for ki in range(32):
                    for pi, (lh, rh) in enumerate([
                            (outT_t[buf][:, ki * 256:ki * 256 + 128],
                             woutc_hl[:, ki * 512:(ki + 1) * 512]),
                            (outT_t[buf][:, ki * 256 + 128:ki * 256 + 256],
                             woutc_hl[:, ki * 512:(ki + 1) * 512]),
                            (outT_t[buf][:, ki * 256:ki * 256 + 128],
                             woutc_hl[:, 16384 + ki * 512:16384 + (ki + 1) * 512])]):
                        mmw = mm(pw[:, :], lh, rh,
                                 start=(ki == 0 and pi == 0), stop=(ki == 31 and pi == 2),
                                 waits=w0 if (ki == 0 and pi == 0) else [],
                                 inc=1 if (ki == 31 and pi == 2) else 0)
                t_ot_free[buf] = mmw
                mmf = None
                for ki in range(NDT):
                    mmf = mm(pf[:, :], xn8[:, ki * T + ti * 128: ki * T + (ti + 1) * 128],
                             q28[:, ki * HID + cb * 512: ki * HID + (cb + 1) * 512],
                             start=(ki == 0), stop=(ki == NDT - 1),
                             waits=[t_q2[NDT - 1], t_xn8[NDT - 1], mmw] if ki == 0 else [],
                             inc=1 if ki == NDT - 1 else 0)
                tf1 = ve(lambda e, pf=pf, buf=buf: e.tensor_scalar(
                    fbuf[buf][:, :], pf[:, :], rscale_col[:, :], None, ALU.mult, ALU.bypass),
                    waits=[mmf, t_rs], inc=1)
                tf2 = ve(lambda e, buf=buf: e.tensor_add(fbuf[buf][:, :], fbuf[buf][:, :],
                                                         xbuf[buf][:, :]),
                         waits=[tf1, t_xd[buf]], inc=1)
                ty = ve(lambda e, pw=pw, buf=buf: e.tensor_add(ybuf[buf][:, :], pw[:, :],
                                                               fbuf[buf][:, :]),
                        waits=[tf2, mmw], inc=1)
                tst = dma_out(y_out[ti * 128:(ti + 1) * 128, cb * 512:(cb + 1) * 512],
                              ybuf[buf][:, :], f"y{buf}", waits=[ty])
                t_ybufree[16 * cb + ti] = tst
                t_y_st.append(tst)
            t_wc = t_wc_done

        final_waits = t_y_st[-4:]


    for sl in S.slots:
        sems[sl] = es.enter_context(nc.semaphore(f"sem_{sl}"))
    blk = es.enter_context(nc.Block())

    @blk.sync
    def _(eng):
        S.replay("sync", eng, sems, self_waits=self_waits)

    @blk.gpsimd
    def _(eng):
        S.replay("gpsimd", eng, sems, self_waits=self_waits)
        for sl, cnt in S.slots.items():
            if S.slot_eng.get(sl) == "gpsimd":
                eng.wait_ge(sems[sl], cnt)

    @blk.tensor
    def _(eng):
        S.replay("tensor", eng, sems, self_waits=self_waits)

    @blk.vector
    def _(eng):
        S.replay("vector", eng, sems, self_waits=self_waits)

    @blk.scalar
    def _(eng):
        S.replay("scalar", eng, sems, self_waits=self_waits)

    es.close()
    return nc


def host_prep(inputs):
    f = np.float32
    Wqkv = np.asarray(inputs["L0_attn_W_qkv"], f)
    Wz = np.asarray(inputs["L0_attn_W_z"], f)
    Wb = np.asarray(inputs["L0_attn_W_b"], f)
    Wa = np.asarray(inputs["L0_attn_W_a"], f)
    Wout = np.asarray(inputs["L0_attn_W_out"], f)
    conv_w = np.asarray(inputs["L0_attn_conv_w"], f)
    common = {}
    wq = Wqkv.reshape(FQ, 128, NDT, 128).transpose(0, 3, 2, 1).reshape(FQ, 128, HID)
    common["wqkv_l"] = np.ascontiguousarray(wq)
    wzt = Wz.reshape(FZ, 128, NDT, 128).transpose(0, 3, 2, 1).reshape(FZ, 128, HID)
    common["wz_l"] = np.ascontiguousarray(wzt)
    Wab = np.concatenate([Wb, Wa], 0)
    common["wab_l"] = np.ascontiguousarray(Wab.reshape(64, NDT, 128).transpose(2, 1, 0).reshape(128, NDT * 64))
    wo = Wout.reshape(4, 512, 32, 128).transpose(0, 2, 3, 1)
    common["wout_l"] = np.ascontiguousarray(wo)
    common["in2"] = np.ascontiguousarray(np.asarray(inputs["L0_ffn_input2"], f))
    common["convw_in"] = np.ascontiguousarray(conv_w.reshape(FQ, 128, 4).transpose(1, 0, 2).reshape(128, FQ * 4))
    common["dtbias_bc"] = np.ascontiguousarray(np.tile(np.asarray(inputs["L0_attn_dt_bias"], f)[None, :], (128, 1)))
    common["alog_bc"] = np.ascontiguousarray(np.tile(np.asarray(inputs["L0_attn_A_log"], f)[None, :], (128, 1)))
    common["normw_bc"] = np.ascontiguousarray(np.tile(np.asarray(inputs["L0_attn_norm_w"], f)[None, :], (128, 1)))
    common["normw_col"] = np.ascontiguousarray(np.asarray(inputs["L0_attn_norm_w"], f)[:, None])
    common["scale_col"] = np.full((128, 1), np.asarray(inputs["L0_ffn_scale"], f), f)
    common["scale1_col"] = np.full((128, 1), np.asarray(inputs["L0_ffn_scale1"], f), f)
    a = np.arange(128)
    common["tri_lu"] = (a[:, None] < a[None, :]).astype(f)
    common["tri_li"] = (a[:, None] <= a[None, :]).astype(f)
    common["ident"] = np.eye(128, dtype=f)
    x = np.asarray(inputs["L0_attn_x"], f)
    maps = []
    for b in range(8):
        m = dict(common)
        m["x_nat"] = np.ascontiguousarray(x[b])
        m["xT"] = np.ascontiguousarray(x[b].T)
        maps.append(m)
    return maps


_NC_CACHE = {}

def kernel(**inputs):
    maps = host_prep(inputs)
    if "nc" not in _NC_CACHE:
        _NC_CACHE["nc"] = build_nc(debug=False)
    nc = _NC_CACHE["nc"]
    res = run_bass_kernel_spmd(nc, maps, core_ids=list(range(8)))
    y = np.stack([np.asarray(res.results[b]["y"]) for b in range(8)], 0)
    return y.astype(np.float32)
